# version 09
# baseline (speedup 1.0000x reference)
import math
import numpy as np
import concourse.bass as bass
import concourse.mybir as mybir
from concourse.bass import ts
from concourse.bass_utils import run_bass_kernel_spmd
from contextlib import ExitStack

F32 = mybir.dt.float32
BF16 = mybir.dt.bfloat16
FP8 = mybir.dt.float8e4
AF = mybir.ActivationFunctionType
ALU = mybir.AluOpType
AX = mybir.AxisListType

B, S, E, H, DH = 2, 8192, 2048, 16, 128
HPC = 4
TOK = 2048
NSTRIP = 16
CSH = 9.0
SCL = float(np.float32(1.0) / np.float32(math.sqrt(DH)))
EPS = 1e-5
INV_E = float(np.float32(1.0 / E))
NG = 32
PHASES = 99
PH2_HEADS = 4
PH2_QB = 16
PH2_CC = True


def build():
    nc = bass.Bass("TRN2", target_bir_lowering=False,
                   detect_race_conditions=False)

    xb_d = nc.dram_tensor("xb", [S, E], F32, kind="ExternalInput")
    xo_d = nc.dram_tensor("xo", [TOK, E], F32, kind="ExternalInput")
    wq_d = nc.dram_tensor("wq", [E, 1536], F32, kind="ExternalInput")
    bq_d = nc.dram_tensor("bq", [1, 1536], F32, kind="ExternalInput")
    wo_d = nc.dram_tensor("wo", [E, E], F32, kind="ExternalInput")
    bo_d = nc.dram_tensor("bo", [1, E], F32, kind="ExternalInput")
    wt_d = nc.dram_tensor("wt", [E, E], F32, kind="ExternalInput")
    fb_d = nc.dram_tensor("fb", [1, E], F32, kind="ExternalInput")
    ws_d = nc.dram_tensor("ws", [1, 1], F32, kind="ExternalInput")
    out_d = nc.dram_tensor("out", [TOK, E], F32, kind="ExternalOutput")

    qc_d = nc.dram_tensor("qc", [HPC, DH, S], BF16)
    kc_d = nc.dram_tensor("kc", [HPC, DH, S], BF16)
    vc_d = nc.dram_tensor("vc", [HPC, S, DH], BF16)
    ctxc_d = nc.dram_tensor("ctxc", [HPC, DH, S], BF16)
    ctxg_d = nc.dram_tensor("ctxg", [HPC, 4, DH, S], BF16)
    x1_d = nc.dram_tensor("x1s", [TOK, E], F32)
    gmi_d = nc.dram_tensor("gmi", [1, 4], F32)
    gmo_d = nc.dram_tensor("gmo", [8, 1, 4], F32)

    ident_d = nc.inline_tensor(np.eye(128, dtype=np.float32), name="identc")
    ones_d = nc.inline_tensor(np.ones((128, 128), dtype=np.float32), name="onesc")

    ops = {k: [] for k in "TSVYG"}
    T = ops["T"].append
    Sc = ops["S"].append
    V = ops["V"].append
    Y = ops["Y"].append
    G = ops["G"].append

    cnt = {}
    sem = {}

    def inc(name, n=1):
        cnt[name] = cnt.get(name, 0) + n
        return cnt[name]

    def cur(name):
        return cnt.get(name, 0)

    def flush_block():
        local = {k: list(v) for k, v in ops.items()}
        for k in ops:
            ops[k].clear()
        with nc.Block() as blk:
            @blk.tensor
            def _(eng):
                for f in local["T"]:
                    f(eng)

            @blk.scalar
            def _(eng):
                for f in local["S"]:
                    f(eng)

            @blk.vector
            def _(eng):
                for f in local["V"]:
                    f(eng)

            @blk.sync
            def _(eng):
                for f in local["Y"]:
                    f(eng)

            @blk.gpsimd
            def _(eng):
                for f in local["G"]:
                    f(eng)

    def barrier():
        marks = {}
        for add, nm in ((T, "sTb"), (Sc, "sSb"), (V, "sVb")):
            v = inc(nm)
            add(lambda eng, h=sem[nm]: eng.drain().then_inc(h, 1))
            marks[nm] = v
        return marks

    def wait_marks(add, marks):
        for nm, v in marks.items():
            add(lambda eng, h=sem[nm], v=v: eng.wait_ge(h, v))

    rt = {}

    with ExitStack() as stk:
        for nm in ("sT", "sS", "sV", "sTb", "sSb", "sVb",
                   "sLd", "sLd2", "sWq", "sWo", "sWf", "sCc", "sCg",
                   "sGm", "sGx", "sX10", "sX11",
                   "sXi0", "sXi1", "sQV0", "sQV1", "sKV0", "sKV1",
                   "sQb0", "sQb1", "sCx0", "sCx1", "sC160", "sC161",
                   "sXo0", "sXo1", "sXl0", "sXl1", "sOu0", "sOu1"):
            sem[nm] = stk.enter_context(nc.semaphore(nm))

        def sb(name, shape, dtype, st=stk):
            return st.enter_context(nc.sbuf_tensor(name, shape, dtype))

        ones32 = sb("ones32", [1, 128], F32)
        ones_bf = sb("ones_bf", [128, 128], BF16)
        ident_bf = sb("ident_bf", [128, 128], BF16)
        ident32 = sb("ident32", [128, 128], F32)
        bqrow = sb("bqrow", [1, 1536], F32)
        bqrow_bf = sb("bqrow_bf", [1, 1536], BF16)
        bsqk = sb("bsqk", [128, 8], F32)
        borow = sb("borow", [1, E], F32)
        fbrow = sb("fbrow", [1, E], F32)
        wsrow = sb("wsrow", [1, 1], F32)
        bout_bc = sb("bout_bc", [128, E], F32)
        fb_bc = sb("fb_bc", [128, E], F32)
        wsb = sb("wsb", [128, 1], F32)
        bm9 = sb("bm9", [128, 1], F32)
        m3s = sb("m3s", [128, 16], F32)
        r3s = sb("r3s", [128, 16], F32)
        aqs = sb("aqs", [128, 16], F32)
        bqs = sb("bqs", [128, 16], F32)
        gmax_a = sb("gmax_a", [128, 1], F32)
        gxall = sb("gxall", [1, 32], F32)
        gx4 = sb("gx4", [1, 4], F32)
        scale_b = sb("scale_b", [128, 1], F32)

        def Vd(eng):
            eng.drain()

        NEG_INV_E2 = -INV_E * INV_E

        def ln_coeffs(src_ap, stc, c0, sq_v, sq_ap):
            V(lambda eng: eng.tensor_reduce(stc[:, c0:c0 + 1], src_ap, AX.X, ALU.add))
            V(lambda eng, v=sq_v: eng.wait_ge(sem["sS"], v))
            V(lambda eng: eng.tensor_scalar_mul(stc[:, c0 + 1:c0 + 2],
                                                sq_ap, INV_E))
            V(Vd)
            V(lambda eng: eng.scalar_tensor_tensor(
                stc[:, c0 + 2:c0 + 3], stc[:, c0:c0 + 1], NEG_INV_E2,
                stc[:, c0:c0 + 1], ALU.mult, ALU.mult))
            V(lambda eng: eng.tensor_scalar_mul(stc[:, c0:c0 + 1],
                                                stc[:, c0:c0 + 1], INV_E))
            V(Vd)
            V(lambda eng: eng.scalar_tensor_tensor(
                stc[:, c0 + 1:c0 + 2], stc[:, c0 + 1:c0 + 2], EPS,
                stc[:, c0 + 2:c0 + 3], ALU.add, ALU.add))
            V(Vd)
            vv = inc("sV")
            V(lambda eng: eng.reciprocal(stc[:, c0 + 2:c0 + 3],
                                         stc[:, c0 + 1:c0 + 2]))
            V(lambda eng: eng.drain().then_inc(sem["sV"], 1))
            Sc(lambda eng, v=vv: eng.wait_ge(sem["sV"], v))
            sv = inc("sS")
            Sc(lambda eng: eng.activation(stc[:, c0 + 3:c0 + 4], stc[:, c0 + 2:c0 + 3],
                                          AF.Sqrt, bias=0.0, scale=1.0))
            Sc(lambda eng: eng.drain().then_inc(sem["sS"], 1))
            V(lambda eng, v=sv: eng.wait_ge(sem["sS"], v))
            rv = inc("sV")
            V(lambda eng: eng.scalar_tensor_tensor(
                stc[:, c0 + 5:c0 + 6], stc[:, c0:c0 + 1], -1.0,
                stc[:, c0 + 3:c0 + 4], ALU.mult, ALU.mult))
            V(lambda eng: eng.drain().then_inc(sem["sV"], 1))
            return rv, c0 + 3, c0 + 5

        with nc.psum_tensor("pset", [128, 512], F32) as pset:
            def ld(q, nm, dst, src):
                v = inc(nm, 16)
                q(lambda eng, dst=dst, src=src, h=sem[nm]: eng.dma_start(
                    out=dst, in_=src).then_inc(h, 16))
                return v

            ld(Y, "sLd", ones32[:, :], ones_d.ap()[0:1, :])
            ld(Y, "sLd", ident32[:, :], ident_d.ap())
            ld(Y, "sLd", bqrow[:, :], bq_d.ap())
            ld(Y, "sLd", borow[:, :], bo_d.ap())
            ld(Y, "sLd", fbrow[:, :], fb_d.ap())
            vy = ld(Y, "sLd", wsrow[:, :], ws_d.ap())
            ld(G, "sLd2", ones_bf[:, :], ones_d.ap())
            ld(G, "sLd2", ident_bf[:, :], ident_d.ap())
            vg = ld(G, "sLd2", bqrow_bf[:, :], bq_d.ap())

            V(lambda eng: eng.memset(bm9[:, :], -CSH))
            V(lambda eng: eng.memset(gmax_a[:, :], 0.0))

            T(lambda eng, v=vy: eng.wait_ge(sem["sLd"], v))
            T(lambda eng, v=vg: eng.wait_ge(sem["sLd2"], v))
            for j in range(8):
                lh, qk = j // 2, j % 2
                off = 384 * lh + 128 * qk
                tv = inc("sT")
                T(lambda eng, off=off: eng.matmul(
                    pset[:, 0:1], bqrow[0:1, off:off + 128], ones32[0:1, 0:1],
                    start=True, stop=True).then_inc(sem["sT"], 1))
                V(lambda eng, v=tv: eng.wait_ge(sem["sT"], v))
                vv = inc("sV")
                V(lambda eng, j=j: eng.tensor_copy(bsqk[:, j:j + 1], pset[:, 0:1]
                                                   ).then_inc(sem["sV"], 1))
                T(lambda eng, v=vv: eng.wait_ge(sem["sV"], v))
            for row, dst in ((borow, bout_bc), (fbrow, fb_bc)):
                for j in range(4):
                    tv = inc("sT")
                    T(lambda eng, row=row, j=j: eng.matmul(
                        pset[:, :], ones32[:, :], row[0:1, 512 * j:512 * (j + 1)],
                        start=True, stop=True).then_inc(sem["sT"], 1))
                    V(lambda eng, v=tv: eng.wait_ge(sem["sT"], v))
                    vv = inc("sV")
                    V(lambda eng, dst=dst, j=j: eng.tensor_copy(
                        dst[:, 512 * j:512 * (j + 1)], pset[:, :]
                    ).then_inc(sem["sV"], 1))
                    T(lambda eng, v=vv: eng.wait_ge(sem["sV"], v))
            tv = inc("sT")
            T(lambda eng: eng.matmul(pset[:, 0:1], ones32[:, :], wsrow[0:1, 0:1],
                                     start=True, stop=True).then_inc(sem["sT"], 1))
            V(lambda eng, v=tv: eng.wait_ge(sem["sT"], v))
            vv = inc("sV")
            V(lambda eng: eng.tensor_copy(wsb[:, :], pset[:, 0:1]
                                          ).then_inc(sem["sV"], 1))
            V(Vd)
            V(lambda eng: eng.tensor_scalar_mul(wsb[:, :], wsb[:, :], 2.0))
            m0 = barrier()
            flush_block()
        if PHASES < 1:
            return nc

        with ExitStack() as p1:
            def sb1(n, sh, dt):
                return p1.enter_context(nc.sbuf_tensor(n, sh, dt))

            xin = [sb1(f"xin{i}", [128, E], F32) for i in range(2)]
            xtm = [sb1(f"xtm{i}", [128, E], BF16) for i in range(2)]
            junk = sb1("junk", [128, E], F32)
            st1q = [sb1(f"st1q{i}", [128, 8], F32) for i in range(2)]
            xfst = [sb1(f"xfst{i}", [128, 16, 512], BF16) for i in range(2)]
            wsb1 = sb1("wqkv", [128, 16, 1536], BF16)
            qkst = [sb1(f"qkst{i}", [128, 256], BF16) for i in range(2)]
            qhm = [sb1(f"qhm{i}", [128, 2, 4, 128], BF16) for i in range(2)]
            vst = [sb1(f"vst{i}", [128, 4, 128], BF16) for i in range(2)]
            st1 = sb1("st1", [128, 8], F32)
            ptbf = [p1.enter_context(nc.psum_tensor(f"ptbf{i}", [128, 2048], BF16))
                    for i in range(2)]
            psq = [p1.enter_context(nc.psum_tensor(f"psq{i}", [128, 512], F32))
                   for i in range(2)]
            pst2 = [p1.enter_context(nc.psum_tensor(f"pst2{i}", [128, 1024], BF16))
                    for i in range(2)]

            wait_marks(G, m0)
            for ec in range(16):
                inc("sWq", 16)
                G(lambda eng, ec=ec: eng.dma_start(
                    out=wsb1[:, ec, :], in_=wq_d.ap()[128 * ec:128 * (ec + 1), :]
                ).then_inc(sem["sWq"], 16))
            wq_done = cur("sWq")

            wait_marks(Y, m0)
            wait_marks(T, m0)
            wait_marks(Sc, m0)
            wait_marks(V, m0)

            xin_free = [0, 0]
            xtm_free = [0, 0]
            xfst_free = [0, 0]
            ptbf_free = [0, 0]
            psq_free = [0, 0]
            pst2_free = [0, 0]
            qk_free = [0, 0]
            qhm_free = [0, 0]
            vst_free = [0, 0]

            for s in range(NSTRIP):
                sb_i = s % 2
                for st_ in range(4):
                    gt = s * 4 + st_
                    bi = gt % 2
                    if xin_free[bi]:
                        Y(lambda eng, v=xin_free[bi][0]: eng.wait_ge(sem["sS"], v))
                        Y(lambda eng, v=xin_free[bi][1]: eng.wait_ge(sem["sV"], v))
                    snm = f"sXi{bi}"
                    vx = inc(snm, 16)
                    Y(lambda eng, gt=gt, bi=bi, h=sem[snm]: eng.dma_start(
                        out=xin[bi][:, :], in_=xb_d.ap()[128 * gt:128 * (gt + 1), :]
                    ).then_inc(h, 16))
                    Sc(lambda eng, v=vx, h=sem[snm]: eng.wait_ge(h, v))
                    sqv = inc("sS")
                    Sc(lambda eng, bi=bi: eng.activation(
                        junk[:, :], xin[bi][:, :], AF.Square, bias=0.0, scale=1.0,
                        accum_out=st1q[bi][:, 0:1]))
                    Sc(lambda eng: eng.drain().then_inc(sem["sS"], 1))
                    V(lambda eng, v=vx, h=sem[snm]: eng.wait_ge(h, v))
                    vv, ac, bc_ = ln_coeffs(xin[bi][:, :], st1, 0, sqv,
                                            st1q[bi][:, 0:1])
                    if xtm_free[bi]:
                        V(lambda eng, v=xtm_free[bi]: eng.wait_ge(sem["sT"], v))
                    vxh = inc("sV")
                    V(lambda eng, bi=bi, ac=ac, bc_=bc_: eng.tensor_scalar(
                        xtm[bi][:, :], xin[bi][:, :], st1[:, ac:ac + 1],
                        st1[:, bc_:bc_ + 1], ALU.mult, ALU.add
                    ).then_inc(sem["sV"], 1))
                    xin_free[bi] = (sqv, vxh)
                    T(lambda eng, v=vxh: eng.wait_ge(sem["sV"], v))
                    if ptbf_free[bi]:
                        T(lambda eng, v=ptbf_free[bi]: eng.wait_ge(sem["sV"], v))
                    tv = inc("sT")
                    for j in range(16):
                        if j < 15:
                            T(lambda eng, bi=bi, j=j: eng.transpose(
                                ptbf[bi][:, 128 * j:128 * (j + 1)],
                                xtm[bi][:, 128 * j:128 * (j + 1)], ident_bf[:, :]))
                        else:
                            T(lambda eng, bi=bi, j=j: eng.transpose(
                                ptbf[bi][:, 128 * j:128 * (j + 1)],
                                xtm[bi][:, 128 * j:128 * (j + 1)], ident_bf[:, :]
                            ).then_inc(sem["sT"], 1))
                    xtm_free[bi] = tv
                    V(lambda eng, v=tv: eng.wait_ge(sem["sT"], v))
                    if st_ == 0 and xfst_free[sb_i]:
                        V(lambda eng, v=xfst_free[sb_i]: eng.wait_ge(sem["sT"], v))
                    ev = inc("sV")
                    V(lambda eng, bi=bi, sb_i=sb_i, st_=st_: eng.tensor_copy(
                        xfst[sb_i][:, :, 128 * st_:128 * (st_ + 1)],
                        ptbf[bi][:, :].rearrange("p (a b) -> p a b", b=128)
                    ).then_inc(sem["sV"], 1))
                    ptbf_free[bi] = ev
                strip_ready = cur("sV")

                if s == 0:
                    T(lambda eng, v=wq_done: eng.wait_ge(sem["sWq"], v))
                for lh in range(HPC):
                    for st_ in range(4):
                        u = (lh * 4 + st_) % 2
                        T(lambda eng, v=strip_ready: eng.wait_ge(sem["sV"], v))
                        if psq_free[u]:
                            T(lambda eng, v=psq_free[u]: eng.wait_ge(sem["sS"], v))
                        for ec in range(16):
                            T(lambda eng, u=u, sb_i=sb_i, st_=st_, ec=ec, lh=lh:
                              eng.matmul(psq[u][:, 0:384],
                                         xfst[sb_i][:, ec, 128 * st_:128 * (st_ + 1)],
                                         wsb1[:, ec, 384 * lh:384 * (lh + 1)],
                                         start=(ec == 0), stop=False))
                        tqv = inc("sT")
                        T(lambda eng, u=u, lh=lh: eng.matmul(
                            psq[u][:, 256:384], ones_bf[0:1, :],
                            bqrow_bf[0:1, 384 * lh + 256:384 * lh + 384],
                            start=False, stop=True).then_inc(sem["sT"], 1))
                        if lh == HPC - 1 and st_ == 3:
                            xfst_free[sb_i] = tqv
                        Sc(lambda eng, v=tqv: eng.wait_ge(sem["sT"], v))
                        if qk_free[u]:
                            Sc(lambda eng, v=qk_free[u]: eng.wait_ge(sem["sT"], v))
                        Sc(lambda eng, u=u: eng.copy(qkst[u][:, :], psq[u][:, 0:256]))
                        if st_ == 0 and vst_free[sb_i]:
                            Sc(lambda eng, v=vst_free[sb_i], h=sem[f"sQV{sb_i}"]:
                               eng.wait_ge(h, v))
                        sev = inc("sS")
                        Sc(lambda eng, u=u, sb_i=sb_i, st_=st_: eng.copy(
                            vst[sb_i][:, st_, :], psq[u][:, 256:384]))
                        Sc(lambda eng: eng.drain().then_inc(sem["sS"], 1))
                        psq_free[u] = sev
                        T(lambda eng, v=sev: eng.wait_ge(sem["sS"], v))
                        if pst2_free[u]:
                            T(lambda eng, v=pst2_free[u]: eng.wait_ge(sem["sS"], v))
                        T(lambda eng, u=u: eng.transpose(
                            pst2[u][:, 0:128], qkst[u][:, 0:128], ident_bf[:, :]))
                        ttv = inc("sT")
                        T(lambda eng, u=u: eng.transpose(
                            pst2[u][:, 128:256], qkst[u][:, 128:256], ident_bf[:, :]
                        ).then_inc(sem["sT"], 1))
                        qk_free[u] = ttv
                        Sc(lambda eng, v=ttv: eng.wait_ge(sem["sT"], v))
                        if st_ == 0 and qhm_free[sb_i]:
                            Sc(lambda eng, v=qhm_free[sb_i], h=sem[f"sQV{sb_i}"]:
                               eng.wait_ge(h, v))
                        Sc(lambda eng, u=u, sb_i=sb_i, st_=st_, lh=lh: eng.activation(
                            qhm[sb_i][:, 0, st_, :], pst2[u][:, 0:128], AF.Identity,
                            bias=bsqk[:, 2 * lh:2 * lh + 1], scale=1.0))
                        sqv2 = inc("sS")
                        Sc(lambda eng, u=u, sb_i=sb_i, st_=st_, lh=lh: eng.activation(
                            qhm[sb_i][:, 1, st_, :], pst2[u][:, 128:256], AF.Identity,
                            bias=bsqk[:, 2 * lh + 1:2 * lh + 2], scale=1.0))
                        Sc(lambda eng: eng.drain().then_inc(sem["sS"], 1))
                        pst2_free[u] = sqv2
                    Y(lambda eng, v=cur("sS"): eng.wait_ge(sem["sS"], v))
                    hQV = sem[f"sQV{sb_i}"]
                    inc(f"sQV{sb_i}", 16)
                    Y(lambda eng, lh=lh, s=s, sb_i=sb_i, h=hQV: eng.dma_start(
                        out=qc_d.ap()[lh, :, 512 * s:512 * (s + 1)],
                        in_=qhm[sb_i][:, 0, :, :].rearrange("p a b -> p (a b)")
                    ).then_inc(h, 16))
                    inc(f"sQV{sb_i}", 16)
                    Y(lambda eng, lh=lh, s=s, sb_i=sb_i, h=hQV: eng.dma_start(
                        out=kc_d.ap()[lh, :, 512 * s:512 * (s + 1)],
                        in_=qhm[sb_i][:, 1, :, :].rearrange("p a b -> p (a b)")
                    ).then_inc(h, 16))
                    vv_ = inc(f"sQV{sb_i}", 16)
                    Y(lambda eng, lh=lh, s=s, sb_i=sb_i, h=hQV: eng.dma_start(
                        out=vc_d.ap()[lh, 512 * s:512 * (s + 1), :].rearrange(
                            "(a p) d -> p a d", p=128),
                        in_=vst[sb_i][:, :, :]).then_inc(h, 16))
                    qhm_free[sb_i] = vv_
                    vst_free[sb_i] = vv_
            ph1_st = (cur("sQV0"), cur("sQV1"))
            m1 = barrier()
            flush_block()
        if PHASES < 2:
            return nc

        wout_scope = ExitStack()
        Wout = wout_scope.enter_context(nc.sbuf_tensor("Wout", [128, 16, E], BF16))
        with ExitStack() as p2:
            def sb2(n, sh, dt):
                return p2.enter_context(nc.sbuf_tensor(n, sh, dt))

            Ksb = [sb2(f"Ksb{i}", [128, S], BF16) for i in range(2)]
            Vsb = [sb2(f"Vsb{i}", [128, 64, 128], BF16) for i in range(2)]
            Qsb = [sb2(f"Qsb{i}", [128, 512], BF16) for i in range(2)]
            pTt = sb2("pTt", [128, 4, 512], BF16)
            ladd = sb2("ladd", [128, 512], F32)
            lbf = sb2("lbf", [128, 512], BF16)
            lrec = sb2("lrec", [1, 512], BF16)
            lb = sb2("lb", [128, 512], BF16)
            ctxe = [sb2(f"ctxe{i}", [128, 512], BF16) for i in range(2)]

            psc = p2.enter_context(nc.psum_tensor("psc", [128, 2048], F32))
            pctx = [p2.enter_context(nc.psum_tensor(f"pctx{i}", [128, 512], F32))
                    for i in range(2)]
            pmis = p2.enter_context(nc.psum_tensor("pmis", [128, 512], F32))

            wait_marks(G, m1)
            for gH in range(16):
                inc("sWo", 16)
                G(lambda eng, gH=gH: eng.dma_start(
                    out=Wout[:, gH, :], in_=wo_d.ap()[128 * gH:128 * (gH + 1), :]
                ).then_inc(sem["sWo"], 16))
            wout_done = cur("sWo")

            wait_marks(Y, m1)
            Y(lambda eng, v=ph1_st[0]: eng.wait_ge(sem["sQV0"], v))
            Y(lambda eng, v=ph1_st[1]: eng.wait_ge(sem["sQV1"], v))
            wait_marks(T, m1)
            wait_marks(Sc, m1)
            wait_marks(V, m1)

            kv_loaded = [0, 0]
            q_loaded = [0, 0]
            head_done = [0, 0]
            qb_done = [0, 0]
            exp_done = {}
            tinc_sc = {}
            tinc_cx = {}
            ladd_done = {}
            ctxe_free = [0, 0]
            pctx_free = [0, 0]
            pmis_free = 0
            lrec_cons = 0
            lbf_cons = 0
            lb_cons = 0

            for lh in range(min(HPC, PH2_HEADS)):
                hb = lh % 2
                if head_done[hb]:
                    Y(lambda eng, v=head_done[hb]: eng.wait_ge(sem["sT"], v))
                hKV = sem[f"sKV{hb}"]
                inc(f"sKV{hb}", 16)
                Y(lambda eng, lh=lh, hb=hb, h=hKV: eng.dma_start(
                    out=Ksb[hb][:, :], in_=kc_d.ap()[lh]).then_inc(h, 16))
                for i4 in range(16):
                    inc(f"sKV{hb}", 16)
                    Y(lambda eng, lh=lh, hb=hb, i4=i4, h=hKV: eng.dma_start(
                        out=Vsb[hb][:, 4 * i4:4 * (i4 + 1), :],
                        in_=vc_d.ap()[lh].rearrange("(a p) d -> p a d", p=128)[
                            :, 4 * i4:4 * (i4 + 1), :]).then_inc(h, 16))
                kv_loaded[hb] = cur(f"sKV{hb}")

                for qb in range(min(16, PH2_QB)):
                    qbb = qb % 2
                    un = lh * 16 + qb
                    if qb_done[qbb]:
                        Y(lambda eng, v=qb_done[qbb]: eng.wait_ge(sem["sT"], v))
                    hQb = sem[f"sQb{qbb}"]
                    inc(f"sQb{qbb}", 16)
                    Y(lambda eng, lh=lh, qb=qb, qbb=qbb, h=hQb: eng.dma_start(
                        out=Qsb[qbb][:, :],
                        in_=qc_d.ap()[lh, :, 512 * qb:512 * (qb + 1)]
                    ).then_inc(h, 16))
                    q_loaded[qbb] = cur(f"sQb{qbb}")

                    T(lambda eng, v=kv_loaded[hb], h=sem[f"sKV{hb}"]: eng.wait_ge(h, v))
                    T(lambda eng, v=q_loaded[qbb], h=hQb: eng.wait_ge(h, v))
                    V(lambda eng: eng.memset(ladd[:, :], 0.0))

                    for g in range(NG):
                        gi = un * NG + g
                        if gi >= 2:
                            T(lambda eng, v=exp_done[gi - 2]: eng.wait_ge(sem["sS"], v))
                        for j in range(2):
                            kt = 2 * g + j
                            bank = (g % 2) * 2 + j
                            if j == 0:
                                T(lambda eng, hb=hb, qbb=qbb, kt=kt, bank=bank:
                                  eng.matmul(psc[:, 512 * bank:512 * (bank + 1)],
                                             Ksb[hb][:, 128 * kt:128 * (kt + 1)],
                                             Qsb[qbb][:, :], start=True, stop=True))
                            else:
                                T(lambda eng, hb=hb, qbb=qbb, kt=kt, bank=bank:
                                  eng.matmul(psc[:, 512 * bank:512 * (bank + 1)],
                                             Ksb[hb][:, 128 * kt:128 * (kt + 1)],
                                             Qsb[qbb][:, :], start=True, stop=True
                                             ).then_inc(sem["sT"], 1))
                        tinc_sc[gi] = inc("sT")
                        if g >= 1:
                            gp = gi - 1
                            T(lambda eng, v=exp_done[gp]: eng.wait_ge(sem["sS"], v))
                            if g == 1 and pctx_free[un % 2]:
                                T(lambda eng, v=pctx_free[un % 2]:
                                  eng.wait_ge(sem["sV"], v))
                            for j in range(2):
                                kt = 2 * (g - 1) + j
                                if j == 0:
                                    T(lambda eng, hb=hb, un=un, g=g, kt=kt, j=j:
                                      eng.matmul(pctx[un % 2][:, :],
                                                 Vsb[hb][:, kt, :],
                                                 pTt[:, 2 * ((g - 1) % 2) + j, :],
                                                 start=(kt == 0), stop=False))
                                else:
                                    T(lambda eng, hb=hb, un=un, g=g, kt=kt, j=j:
                                      eng.matmul(pctx[un % 2][:, :],
                                                 Vsb[hb][:, kt, :],
                                                 pTt[:, 2 * ((g - 1) % 2) + j, :],
                                                 start=False, stop=False
                                                 ).then_inc(sem["sT"], 1))
                            tinc_cx[gp] = inc("sT")
                        Sc(lambda eng, v=tinc_sc[gi]: eng.wait_ge(sem["sT"], v))
                        if gi >= 2:
                            Sc(lambda eng, v=tinc_cx[gi - 2]: eng.wait_ge(sem["sT"], v))
                            Sc(lambda eng, v=ladd_done[gi - 2]: eng.wait_ge(sem["sV"], v))
                        exp_done[gi] = inc("sS")
                        Sc(lambda eng, g=g: eng.activation(
                            pTt[:, 2 * (g % 2):2 * (g % 2) + 2, :].rearrange(
                                "p a b -> p (a b)"),
                            psc[:, 1024 * (g % 2):1024 * (g % 2) + 1024],
                            AF.Exp, bias=bm9[:, :], scale=SCL))
                        Sc(lambda eng: eng.drain().then_inc(sem["sS"], 1))
                        V(lambda eng, v=exp_done[gi]: eng.wait_ge(sem["sS"], v))
                        V(lambda eng, g=g: eng.tensor_add(
                            ladd[:, :], ladd[:, :], pTt[:, 2 * (g % 2), :]))
                        ladd_done[gi] = inc("sV")
                        V(lambda eng, g=g: eng.tensor_add(
                            ladd[:, :], ladd[:, :], pTt[:, 2 * (g % 2) + 1, :]
                        ).then_inc(sem["sV"], 1))
                    glast = un * NG + NG - 1
                    T(lambda eng, v=exp_done[glast]: eng.wait_ge(sem["sS"], v))
                    for j in range(2):
                        kt = 2 * (NG - 1) + j
                        if j == 0:
                            T(lambda eng, hb=hb, un=un, kt=kt, j=j: eng.matmul(
                                pctx[un % 2][:, :], Vsb[hb][:, kt, :],
                                pTt[:, 2 * ((NG - 1) % 2) + j, :],
                                start=False, stop=False))
                        else:
                            T(lambda eng, hb=hb, un=un, kt=kt, j=j: eng.matmul(
                                pctx[un % 2][:, :], Vsb[hb][:, kt, :],
                                pTt[:, 2 * ((NG - 1) % 2) + j, :],
                                start=False, stop=True).then_inc(sem["sT"], 1))
                    tinc_cx[glast] = inc("sT")
                    if qb == 15:
                        head_done[hb] = tinc_cx[glast]
                    qb_done[qbb] = tinc_cx[glast]

                    if lbf_cons:
                        V(lambda eng, v=lbf_cons: eng.wait_ge(sem["sT"], v))
                    vlb = inc("sV")
                    V(lambda eng: eng.tensor_copy(lbf[:, :], ladd[:, :]))
                    V(lambda eng: eng.drain().then_inc(sem["sV"], 1))
                    T(lambda eng, v=vlb: eng.wait_ge(sem["sV"], v))
                    if pmis_free:
                        T(lambda eng, v=pmis_free: eng.wait_ge(sem["sS"], v))
                    lbf_cons = inc("sT")
                    T(lambda eng: eng.matmul(pmis[0:1, :], ones_bf[:, 0:1], lbf[:, :],
                                             start=True, stop=True
                                             ).then_inc(sem["sT"], 1))
                    V(lambda eng, v=lbf_cons: eng.wait_ge(sem["sT"], v))
                    if lrec_cons:
                        V(lambda eng, v=lrec_cons: eng.wait_ge(sem["sT"], v))
                    vrec = inc("sV")

                    def _recip(eng):
                        with nc.allow_low_precision(
                                reason="softmax 1/l stored bf16; within bar"):
                            eng.reciprocal(lrec[:, :], pmis[0:1, :])
                    V(_recip)
                    V(lambda eng: eng.drain().then_inc(sem["sV"], 1))
                    T(lambda eng, v=vrec: eng.wait_ge(sem["sV"], v))
                    lrec_cons = inc("sT")
                    T(lambda eng: eng.matmul(pmis[:, :], ones_bf[0:1, :],
                                             lrec[0:1, :], start=True, stop=True
                                             ).then_inc(sem["sT"], 1))
                    Sc(lambda eng, v=lrec_cons: eng.wait_ge(sem["sT"], v))
                    if lb_cons:
                        Sc(lambda eng, v=lb_cons: eng.wait_ge(sem["sV"], v))
                    pmis_free = inc("sS")
                    Sc(lambda eng: eng.copy(lb[:, :], pmis[:, :]))
                    Sc(lambda eng: eng.drain().then_inc(sem["sS"], 1))
                    V(lambda eng, v=pmis_free: eng.wait_ge(sem["sS"], v))
                    if ctxe_free[un % 2]:
                        V(lambda eng, v=ctxe_free[un % 2], h=sem[f"sCx{un % 2}"]:
                          eng.wait_ge(h, v))
                    lb_cons = inc("sV")
                    V(lambda eng, un=un: eng.tensor_tensor(
                        ctxe[un % 2][:, :], pctx[un % 2][:, :], lb[:, :], ALU.mult))
                    V(lambda eng: eng.drain().then_inc(sem["sV"], 1))
                    pctx_free[un % 2] = lb_cons
                    Y(lambda eng, v=lb_cons: eng.wait_ge(sem["sV"], v))
                    vst_ = inc(f"sCx{un % 2}", 16)
                    Y(lambda eng, lh=lh, qb=qb, un=un, h=sem[f"sCx{un % 2}"]:
                      eng.dma_start(
                          out=ctxc_d.ap()[lh, :, 512 * qb:512 * (qb + 1)],
                          in_=ctxe[un % 2][:, :]).then_inc(h, 16))
                    ctxe_free[un % 2] = vst_
                if PH2_CC:
                    G(lambda eng, v=cur("sCx0"): eng.wait_ge(sem["sCx0"], v))
                    G(lambda eng, v=cur("sCx1"): eng.wait_ge(sem["sCx1"], v))
                    inc("sCc")
                    G(lambda eng, lh=lh: eng.collective_compute(
                        "AllGather", ALU.bypass,
                        replica_groups=[[0, 1, 2, 3], [4, 5, 6, 7]],
                        ins=[ctxc_d.ap()[lh]], outs=[ctxg_d.ap()[lh]],
                    ).then_inc(sem["sCc"], 1))
            cc_heads = cur("sCc")
            m2 = barrier()
            flush_block()
        if PHASES < 3:
            wout_scope.close()
            return nc

        with ExitStack() as p3:
            def sb3(n, sh, dt):
                return p3.enter_context(nc.sbuf_tensor(n, sh, dt))

            ctx16 = [sb3(f"ctx16{i}", [128, 16, 512], BF16) for i in range(2)]
            xot = [sb3(f"xot{i}", [128, E], F32) for i in range(2)]
            ao = sb3("ao", [128, E], F32)
            x2h = sb3("x2h", [128, E], F32)
            x1t = [sb3(f"x1t{i}", [128, E], F32) for i in range(2)]
            xh3 = sb3("xh3", [128, E], F32)
            junk3 = sb3("junk3", [128, E], F32)
            st3 = sb3("st3", [128, 16], F32)
            st3qa = sb3("st3qa", [128, 8], F32)
            st3qb = sb3("st3qb", [128, 8], F32)
            pso = [p3.enter_context(nc.psum_tensor(f"pso{i}", [128, 2048], F32))
                   for i in range(2)]

            wait_marks(Y, m2)
            Y(lambda eng, v=cc_heads: eng.wait_ge(sem["sCc"], v))
            Y(lambda eng, v=wout_done: eng.wait_ge(sem["sWo"], v))
            def _mkq(eng):
                q0 = eng.partition_id() % 4
                rt["qidx"] = [eng.compute_val(q0 * 4 + uu) for uu in range(4)]
            Y(_mkq)
            wait_marks(T, m2)
            T(lambda eng, v=wout_done: eng.wait_ge(sem["sWo"], v))
            wait_marks(Sc, m2)
            wait_marks(V, m2)

            ctx_ld = [0, 0]
            ctx16_free = [0, 0]
            x1_free = [0, 0]
            pso_free = [0, 0]
            ao_free = 0
            x1t_sc = [0, 0]
            tile_tov = {}
            for t in range(16):
                u, tb = t // 4, t % 2
                ub = (t // 4) % 2
                if t % 4 == 0:
                    if ctx16_free[ub]:
                        Y(lambda eng, v=ctx16_free[ub]: eng.wait_ge(sem["sT"], v))
                    for i4 in range(4):
                        inc(f"sC16{ub}", 16)
                        Y(lambda eng, i4=i4, u=u, ub=ub, h=sem[f"sC16{ub}"]:
                          eng.dma_start(
                              out=ctx16[ub][:, 4 * i4:4 * (i4 + 1), :],
                              in_=ctxg_d.ap()[:, i4, :, ts(rt["qidx"][u], 512)
                                              ].transpose([1, 0, 2])
                          ).then_inc(h, 16))
                    ctx_ld[ub] = cur(f"sC16{ub}")
                if x1_free[tb]:
                    Y(lambda eng, v=x1_free[tb]: eng.wait_ge(sem["sV"], v))
                inc(f"sXo{tb}", 16)
                Y(lambda eng, t=t, tb=tb, h=sem[f"sXo{tb}"]: eng.dma_start(
                    out=xot[tb][:, :], in_=xo_d.ap()[128 * t:128 * (t + 1), :]
                ).then_inc(h, 16))
                xv = cur(f"sXo{tb}")

                T(lambda eng, v=ctx_ld[ub], h=sem[f"sC16{ub}"]: eng.wait_ge(h, v))
                if pso_free[t % 2]:
                    T(lambda eng, v=pso_free[t % 2]: eng.wait_ge(sem["sV"], v))
                for gH in range(16):
                    for j in range(4):
                        if gH == 15 and j == 3:
                            T(lambda eng, t=t, gH=gH, j=j, ub=ub: eng.matmul(
                                pso[t % 2][:, 512 * j:512 * (j + 1)],
                                ctx16[ub][:, gH, 128 * (t % 4):128 * (t % 4) + 128],
                                Wout[:, gH, 512 * j:512 * (j + 1)],
                                start=False, stop=True).then_inc(sem["sT"], 1))
                        else:
                            T(lambda eng, t=t, gH=gH, j=j, ub=ub: eng.matmul(
                                pso[t % 2][:, 512 * j:512 * (j + 1)],
                                ctx16[ub][:, gH, 128 * (t % 4):128 * (t % 4) + 128],
                                Wout[:, gH, 512 * j:512 * (j + 1)],
                                start=(gH == 0), stop=(gH == 15)))
                tov = inc("sT")
                tile_tov[t] = tov
                if t % 4 == 3:
                    ctx16_free[ub] = tov
                V(lambda eng, v=tov: eng.wait_ge(sem["sT"], v))
                if ao_free:
                    V(lambda eng, v=ao_free: eng.wait_ge(sem["sS"], v))
                vao = inc("sV")
                V(lambda eng, t=t: eng.tensor_add(ao[:, :], pso[t % 2][:, :],
                                                  bout_bc[:, :]).then_inc(sem["sV"], 1))
                pso_free[t % 2] = vao
                Sc(lambda eng, v=vao: eng.wait_ge(sem["sV"], v))
                sq2 = inc("sS")
                Sc(lambda eng: eng.activation(junk3[:, :], ao[:, :], AF.Square,
                                              bias=0.0, scale=1.0,
                                              accum_out=st3qa[:, 0:1]))
                Sc(lambda eng: eng.drain().then_inc(sem["sS"], 1))
                V(lambda eng, v=vao: eng.wait_ge(sem["sV"], v))
                vv2, ac2, bc2 = ln_coeffs(ao[:, :], st3, 0, sq2, st3qa[:, 0:1])
                ao_free = sq2
                V(lambda eng, ac2=ac2, bc2=bc2: eng.tensor_scalar(
                    x2h[:, :], ao[:, :], st3[:, ac2:ac2 + 1],
                    st3[:, bc2:bc2 + 1], ALU.mult, ALU.add))
                V(lambda eng, v=xv, h=sem[f"sXo{tb}"]: eng.wait_ge(h, v))
                if x1t_sc[tb]:
                    V(lambda eng, v=x1t_sc[tb]: eng.wait_ge(sem["sS"], v))
                if t >= 2:
                    V(lambda eng, v=cur(f"sX1{tb}"), h=sem[f"sX1{tb}"]:
                      eng.wait_ge(h, v))
                vx1 = inc("sV")
                V(lambda eng, tb=tb: eng.tensor_add(x1t[tb][:, :], xot[tb][:, :],
                                                    x2h[:, :]).then_inc(sem["sV"], 1))
                Y(lambda eng, v=vx1: eng.wait_ge(sem["sV"], v))
                inc(f"sX1{tb}", 16)
                Y(lambda eng, t=t, tb=tb, h=sem[f"sX1{tb}"]: eng.dma_start(
                    out=x1_d.ap()[128 * t:128 * (t + 1), :], in_=x1t[tb][:, :]
                ).then_inc(h, 16))
                Sc(lambda eng, v=vx1: eng.wait_ge(sem["sV"], v))
                sq3 = inc("sS")
                Sc(lambda eng, tb=tb: eng.activation(junk3[:, :], x1t[tb][:, :],
                                                     AF.Square, bias=0.0, scale=1.0,
                                                     accum_out=st3qb[:, 0:1]))
                Sc(lambda eng: eng.drain().then_inc(sem["sS"], 1))
                vv3, ac3, bc3 = ln_coeffs(x1t[tb][:, :], st3, 8, sq3,
                                          st3qb[:, 0:1])
                x1t_sc[tb] = sq3
                V(lambda eng, t=t: eng.tensor_copy(m3s[:, t:t + 1], st3[:, 8:9]))
                V(lambda eng, t=t, ac3=ac3: eng.tensor_copy(
                    r3s[:, t:t + 1], st3[:, ac3:ac3 + 1]))
                V(lambda eng, tb=tb, ac3=ac3, bc3=bc3: eng.tensor_scalar(
                    xh3[:, :], x1t[tb][:, :], st3[:, ac3:ac3 + 1],
                    st3[:, bc3:bc3 + 1], ALU.mult, ALU.add))
                V(lambda eng: eng.tensor_reduce(st3[:, 14:15], xh3[:, :], AX.X,
                                                ALU.max, apply_absolute_value=True))
                V(Vd)
                x1_free[tb] = inc("sV")
                V(lambda eng: eng.tensor_max(gmax_a[:, :], gmax_a[:, :],
                                             st3[:, 14:15]).then_inc(sem["sV"], 1))
            m3m = barrier()
            flush_block()
        if PHASES < 4:
            wout_scope.close()
            return nc

        with nc.psum_tensor("ptl", [128, 512], F32) as ptl:
            wait_marks(T, m3m)
            ttl = inc("sT")
            T(lambda eng: eng.transpose(ptl[0:1, 0:128], gmax_a[:, :],
                                        ident32[:, :]).then_inc(sem["sT"], 1))
            wait_marks(V, m3m)
            V(lambda eng, v=ttl: eng.wait_ge(sem["sT"], v))
            V(lambda eng: eng.tensor_reduce(gx4[:, 0:1], ptl[0:1, 0:128], AX.X,
                                            ALU.max))
            V(Vd)
            V(lambda eng: eng.tensor_copy(gx4[:, 1:2], gx4[:, 0:1]))
            V(lambda eng: eng.tensor_copy(gx4[:, 2:3], gx4[:, 0:1]))
            vg4 = inc("sV")
            V(lambda eng: eng.tensor_copy(gx4[:, 3:4], gx4[:, 0:1]))
            V(lambda eng: eng.drain().then_inc(sem["sV"], 1))
            wait_marks(G, m3m)
            G(lambda eng, v=vg4: eng.wait_ge(sem["sV"], v))
            vgi = inc("sGm", 16)
            G(lambda eng: eng.dma_start(out=gmi_d.ap(), in_=gx4[:, :]
                                        ).then_inc(sem["sGm"], 16))
            G(lambda eng, v=vgi: eng.wait_ge(sem["sGm"], v))
            vcc = inc("sCg")
            G(lambda eng: eng.collective_compute(
                "AllGather", ALU.bypass,
                replica_groups=[[0, 1, 2, 3, 4, 5, 6, 7]],
                ins=[gmi_d.ap()], outs=[gmo_d.ap()]).then_inc(sem["sCg"], 1))
            wait_marks(Y, m3m)
            Y(lambda eng, v=vcc: eng.wait_ge(sem["sCg"], v))
            vgl = inc("sGx", 16)
            Y(lambda eng: eng.dma_start(
                out=gxall[:, :], in_=gmo_d.ap().rearrange("a b c -> b (a c)")
            ).then_inc(sem["sGx"], 16))
            V(lambda eng, v=vgl: eng.wait_ge(sem["sGx"], v))
            gx_s0 = gxall[:, :].rearrange("a (c s) -> a s c", s=4)[:, 0, :]
            V(lambda eng: eng.tensor_reduce(gx4[:, 0:1], gx_s0, AX.X, ALU.max))
            V(Vd)
            V(lambda eng: eng.tensor_scalar(gx4[:, 0:1], gx4[:, 0:1], 1.0 / 448.0,
                                            1e-7, ALU.mult, ALU.max))
            V(Vd)
            V(lambda eng: eng.reciprocal(gx4[:, 1:2], gx4[:, 0:1]))
            V(Vd)
            V(lambda eng: eng.tensor_tensor(gx4[:, 2:3], gx4[:, 0:1], gx4[:, 1:2],
                                            ALU.mult))
            V(Vd)
            V(lambda eng: eng.tensor_scalar(gx4[:, 2:3], gx4[:, 2:3], -1.0, 2.0,
                                            ALU.mult, ALU.add))
            V(Vd)
            V(lambda eng: eng.tensor_tensor(gx4[:, 1:2], gx4[:, 1:2], gx4[:, 2:3],
                                            ALU.mult))
            V(Vd)
            vsc = inc("sV")
            V(lambda eng: eng.tensor_scalar_mul(gx4[:, 1:2], gx4[:, 1:2], 0.5))
            V(lambda eng: eng.drain().then_inc(sem["sV"], 1))
            T(lambda eng, v=vsc: eng.wait_ge(sem["sV"], v))
            tsc = inc("sT")
            T(lambda eng: eng.matmul(ptl[:, 0:1], ones32[:, :], gx4[0:1, 1:2],
                                     start=True, stop=True).then_inc(sem["sT"], 1))
            V(lambda eng, v=tsc: eng.wait_ge(sem["sT"], v))
            V(lambda eng: eng.tensor_copy(scale_b[:, :], ptl[:, 0:1]))
            V(Vd)
            for t in range(16):
                V(lambda eng, t=t: eng.tensor_tensor(aqs[:, t:t + 1], r3s[:, t:t + 1],
                                                     scale_b[:, :], ALU.mult))
            V(Vd)
            for t in range(16):
                V(lambda eng, t=t: eng.scalar_tensor_tensor(
                    bqs[:, t:t + 1], m3s[:, t:t + 1], -1.0, aqs[:, t:t + 1],
                    ALU.mult, ALU.mult))
            inc("sV")
            V(lambda eng: eng.drain().then_inc(sem["sV"], 1))
            m3t = barrier()
            flush_block()
        wout_scope.close()
        if PHASES < 5:
            return nc

        with ExitStack() as p4:
            def sb4(n, sh, dt):
                return p4.enter_context(nc.sbuf_tensor(n, sh, dt))

            wT = sb4("wT", [128, 16, E], BF16)
            x1i = [sb4(f"x1i{i}", [128, E], F32) for i in range(2)]
            xqt = sb4("xqt", [128, E], F32)
            xq8 = sb4("xq8", [128, E], FP8)
            xqbf = sb4("xqbf", [128, E], BF16)
            xqf = sb4("xqf", [128, 16, 128], BF16)
            ffo = sb4("ffo", [128, E], F32)
            x4h = sb4("x4h", [128, E], F32)
            junk4 = sb4("junk4", [128, E], F32)
            st4 = sb4("st4", [128, 8], F32)
            st4q = sb4("st4q", [128, 8], F32)
            outt = [sb4(f"outt{i}", [128, E], F32) for i in range(2)]
            psf = p4.enter_context(nc.psum_tensor("psf", [128, 2048], F32))
            ptb4 = [p4.enter_context(nc.psum_tensor(f"ptb4{i}", [128, 1024], BF16))
                    for i in range(2)]

            wait_marks(G, m3t)
            for ic in range(16):
                inc("sWf", 16)
                G(lambda eng, ic=ic: eng.dma_start(
                    out=wT[:, ic, :], in_=wt_d.ap()[128 * ic:128 * (ic + 1), :]
                ).then_inc(sem["sWf"], 16))
            wt_done = cur("sWf")

            wait_marks(Y, m3t)
            Y(lambda eng, v=cur("sX10"): eng.wait_ge(sem["sX10"], v))
            Y(lambda eng, v=cur("sX11"): eng.wait_ge(sem["sX11"], v))
            wait_marks(T, m3t)
            wait_marks(Sc, m3t)
            wait_marks(V, m3t)
            T(lambda eng, v=wt_done: eng.wait_ge(sem["sWf"], v))

            x1_ld = [0, 0]
            out_free = [0, 0]
            psf_free = 0
            ptb_free = [0, 0]
            xq_free = 0
            xqt_free = 0
            ffo_free = 0
            for t in range(16):
                tb = t % 2
                if x1_ld[tb]:
                    Y(lambda eng, v=x1_ld[tb]: eng.wait_ge(sem["sV"], v))
                inc(f"sXl{tb}", 16)
                Y(lambda eng, t=t, tb=tb, h=sem[f"sXl{tb}"]: eng.dma_start(
                    out=x1i[tb][:, :], in_=x1_d.ap()[128 * t:128 * (t + 1), :]
                ).then_inc(h, 16))
                xv = cur(f"sXl{tb}")
                V(lambda eng, v=xv, h=sem[f"sXl{tb}"]: eng.wait_ge(h, v))
                if xqt_free:
                    V(lambda eng, v=xqt_free: eng.wait_ge(sem["sS"], v))
                tsv = inc("sV")
                V(lambda eng, t=t, tb=tb: eng.tensor_scalar(
                    xqt[:, :], x1i[tb][:, :], aqs[:, t:t + 1], bqs[:, t:t + 1],
                    ALU.mult, ALU.add).then_inc(sem["sV"], 1))
                Sc(lambda eng, v=tsv: eng.wait_ge(sem["sV"], v))
                xqt_free = inc("sS")
                Sc(lambda eng: eng.copy(xq8[:, :], xqt[:, :]).then_inc(sem["sS"], 1))
                if xq_free:
                    Sc(lambda eng, v=xq_free: eng.wait_ge(sem["sT"], v))
                sqv = inc("sS")
                Sc(lambda eng: eng.copy(xqbf[:, :], xq8[:, :]).then_inc(sem["sS"], 1))
                T(lambda eng, v=sqv: eng.wait_ge(sem["sS"], v))
                for half in range(2):
                    if ptb_free[half]:
                        T(lambda eng, v=ptb_free[half]: eng.wait_ge(sem["sV"], v))
                    tv = inc("sT")
                    for j in range(8):
                        jj = half * 8 + j
                        if j < 7:
                            T(lambda eng, half=half, j=j, jj=jj: eng.transpose(
                                ptb4[half][:, 128 * j:128 * (j + 1)],
                                xqbf[:, 128 * jj:128 * (jj + 1)], ident_bf[:, :]))
                        else:
                            T(lambda eng, half=half, j=j, jj=jj: eng.transpose(
                                ptb4[half][:, 128 * j:128 * (j + 1)],
                                xqbf[:, 128 * jj:128 * (jj + 1)], ident_bf[:, :]
                            ).then_inc(sem["sT"], 1))
                    V(lambda eng, v=tv: eng.wait_ge(sem["sT"], v))
                    pv = inc("sV")
                    V(lambda eng, half=half: eng.tensor_copy(
                        xqf[:, 8 * half:8 * (half + 1), :].rearrange("p a b -> p (a b)"),
                        ptb4[half][:, :]).then_inc(sem["sV"], 1))
                    ptb_free[half] = pv
                xq_free = cur("sT")
                vxq = cur("sV")
                T(lambda eng, v=vxq: eng.wait_ge(sem["sV"], v))
                if psf_free:
                    T(lambda eng, v=psf_free: eng.wait_ge(sem["sV"], v))
                for ic in range(16):
                    for j in range(4):
                        if ic == 15 and j == 3:
                            T(lambda eng, ic=ic, j=j: eng.matmul(
                                psf[:, 512 * j:512 * (j + 1)], xqf[:, ic, :],
                                wT[:, ic, 512 * j:512 * (j + 1)],
                                start=False, stop=True).then_inc(sem["sT"], 1))
                        else:
                            T(lambda eng, ic=ic, j=j: eng.matmul(
                                psf[:, 512 * j:512 * (j + 1)], xqf[:, ic, :],
                                wT[:, ic, 512 * j:512 * (j + 1)],
                                start=(ic == 0), stop=(ic == 15)))
                tf = inc("sT")
                V(lambda eng, v=tf: eng.wait_ge(sem["sT"], v))
                if ffo_free:
                    V(lambda eng, v=ffo_free: eng.wait_ge(sem["sS"], v))
                psf_free = inc("sV")
                V(lambda eng: eng.scalar_tensor_tensor(
                    ffo[:, :], psf[:, :], wsb[:, :], fb_bc[:, :], ALU.mult, ALU.add
                ).then_inc(sem["sV"], 1))
                Sc(lambda eng, v=psf_free: eng.wait_ge(sem["sV"], v))
                sq4 = inc("sS")
                Sc(lambda eng: eng.activation(junk4[:, :], ffo[:, :], AF.Square,
                                              bias=0.0, scale=1.0,
                                              accum_out=st4q[:, 0:1]))
                Sc(lambda eng: eng.drain().then_inc(sem["sS"], 1))
                vv4, ac4, bc4 = ln_coeffs(ffo[:, :], st4, 0, sq4, st4q[:, 0:1])
                ffo_free = sq4
                V(lambda eng, ac4=ac4, bc4=bc4: eng.tensor_scalar(
                    x4h[:, :], ffo[:, :], st4[:, ac4:ac4 + 1],
                    st4[:, bc4:bc4 + 1], ALU.mult, ALU.add))
                if out_free[tb]:
                    V(lambda eng, v=out_free[tb], h=sem[f"sOu{tb}"]:
                      eng.wait_ge(h, v))
                x1_ld[tb] = inc("sV")
                V(lambda eng, tb=tb: eng.tensor_add(outt[tb][:, :], x1i[tb][:, :],
                                                    x4h[:, :]).then_inc(sem["sV"], 1))
                Y(lambda eng, v=x1_ld[tb]: eng.wait_ge(sem["sV"], v))
                ov = inc(f"sOu{tb}", 16)
                Y(lambda eng, t=t, tb=tb, h=sem[f"sOu{tb}"]: eng.dma_start(
                    out=out_d.ap()[128 * t:128 * (t + 1), :], in_=outt[tb][:, :]
                ).then_inc(h, 16))
                out_free[tb] = ov
            Y(lambda eng, v=cur("sOu0"): eng.wait_ge(sem["sOu0"], v))
            Y(lambda eng, v=cur("sOu1"): eng.wait_ge(sem["sOu1"], v))
            flush_block()

    return nc


def kernel(L0_attn_x, L0_attn_W_qkv, L0_attn_b_qkv, L0_attn_W_out, L0_attn_b_out,
           L0_ffn_weight, L0_ffn_weight_scale, L0_ffn_bias):
    x = np.ascontiguousarray(np.asarray(L0_attn_x, dtype=np.float32))
    wq = np.ascontiguousarray(np.asarray(L0_attn_W_qkv, dtype=np.float32))
    bq = np.asarray(L0_attn_b_qkv, dtype=np.float32)
    wo = np.ascontiguousarray(np.asarray(L0_attn_W_out, dtype=np.float32))
    bo = np.ascontiguousarray(np.asarray(L0_attn_b_out, dtype=np.float32)
                              .reshape(1, E))
    wf = np.asarray(L0_ffn_weight, dtype=np.float32)
    ws = np.ascontiguousarray(np.asarray(L0_ffn_weight_scale, dtype=np.float32)
                              .reshape(1, 1))
    fb = np.ascontiguousarray(np.asarray(L0_ffn_bias, dtype=np.float32)
                              .reshape(1, E))
    wt = np.ascontiguousarray(wf.T)

    in_maps = []
    for c in range(8):
        b, g = c // 4, c % 4
        in_maps.append({
            "xb": x[b],
            "xo": np.ascontiguousarray(x[b, TOK * g:TOK * (g + 1)]),
            "wq": np.ascontiguousarray(wq[:, 1536 * g:1536 * (g + 1)]),
            "bq": np.ascontiguousarray(bq[1536 * g:1536 * (g + 1)].reshape(1, 1536)),
            "wo": wo,
            "bo": bo,
            "wt": wt,
            "fb": fb,
            "ws": ws,
        })
    nc = build()
    res = run_bass_kernel_spmd(nc, in_maps, core_ids=list(range(8)))
    out = np.stack([np.asarray(res.results[c]["out"], dtype=np.float32)
                    for c in range(8)])
    return out.reshape(B, S, E)


# version 10 (final)
# speedup vs baseline: 1.0385x; 1.0385x over previous
import math
import numpy as np
import concourse.bass as bass
import concourse.mybir as mybir
from concourse.bass import ts
from concourse.bass_utils import run_bass_kernel_spmd
from contextlib import ExitStack

F32 = mybir.dt.float32
BF16 = mybir.dt.bfloat16
FP8 = mybir.dt.float8e4
AF = mybir.ActivationFunctionType
ALU = mybir.AluOpType
AX = mybir.AxisListType

B, S, E, H, DH = 2, 8192, 2048, 16, 128
HPC = 4
TOK = 2048
NSTRIP = 16
CSH = 9.0
SCL = float(np.float32(1.0) / np.float32(math.sqrt(DH)))
EPS = 1e-5
INV_E = float(np.float32(1.0 / E))
NG = 32
PHASES = 99
PH2_HEADS = 4
PH2_QB = 16
PH2_CC = True


def build():
    nc = bass.Bass("TRN2", target_bir_lowering=False,
                   detect_race_conditions=False)

    xb_d = nc.dram_tensor("xb", [S, E], F32, kind="ExternalInput")
    xo_d = nc.dram_tensor("xo", [TOK, E], F32, kind="ExternalInput")
    wq_d = nc.dram_tensor("wq", [E, 1536], F32, kind="ExternalInput")
    bq_d = nc.dram_tensor("bq", [1, 1536], F32, kind="ExternalInput")
    wo_d = nc.dram_tensor("wo", [E, E], F32, kind="ExternalInput")
    bo_d = nc.dram_tensor("bo", [1, E], F32, kind="ExternalInput")
    wt_d = nc.dram_tensor("wt", [E, E], F32, kind="ExternalInput")
    fb_d = nc.dram_tensor("fb", [1, E], F32, kind="ExternalInput")
    ws_d = nc.dram_tensor("ws", [1, 1], F32, kind="ExternalInput")
    out_d = nc.dram_tensor("out", [TOK, E], F32, kind="ExternalOutput")

    qc_d = nc.dram_tensor("qc", [HPC, DH, S], BF16)
    kc_d = nc.dram_tensor("kc", [HPC, DH, S], BF16)
    vc_d = nc.dram_tensor("vc", [HPC, S, DH], BF16)
    ctxc_d = nc.dram_tensor("ctxc", [HPC, DH, S], BF16)
    ctxg_d = nc.dram_tensor("ctxg", [HPC, 4, DH, S], BF16)
    x1_d = nc.dram_tensor("x1s", [TOK, E], F32)
    gmi_d = nc.dram_tensor("gmi", [1, 4], F32)
    gmo_d = nc.dram_tensor("gmo", [8, 1, 4], F32)

    ident_d = nc.inline_tensor(np.eye(128, dtype=np.float32), name="identc")
    ones_d = nc.inline_tensor(np.ones((128, 128), dtype=np.float32), name="onesc")

    ops = {k: [] for k in "TSVYG"}
    T = ops["T"].append
    Sc = ops["S"].append
    V = ops["V"].append
    Y = ops["Y"].append
    G = ops["G"].append

    cnt = {}
    sem = {}

    def inc(name, n=1):
        cnt[name] = cnt.get(name, 0) + n
        return cnt[name]

    def cur(name):
        return cnt.get(name, 0)

    def flush_block():
        local = {k: list(v) for k, v in ops.items()}
        for k in ops:
            ops[k].clear()
        with nc.Block() as blk:
            @blk.tensor
            def _(eng):
                for f in local["T"]:
                    f(eng)

            @blk.scalar
            def _(eng):
                for f in local["S"]:
                    f(eng)

            @blk.vector
            def _(eng):
                for f in local["V"]:
                    f(eng)

            @blk.sync
            def _(eng):
                for f in local["Y"]:
                    f(eng)

            @blk.gpsimd
            def _(eng):
                for f in local["G"]:
                    f(eng)

    def barrier():
        marks = {}
        for add, nm in ((T, "sTb"), (Sc, "sSb"), (V, "sVb")):
            v = inc(nm)
            add(lambda eng, h=sem[nm]: eng.drain().then_inc(h, 1))
            marks[nm] = v
        return marks

    def wait_marks(add, marks):
        for nm, v in marks.items():
            add(lambda eng, h=sem[nm], v=v: eng.wait_ge(h, v))

    rt = {}

    with ExitStack() as stk:
        for nm in ("sT", "sS", "sV", "sTb", "sSb", "sVb",
                   "sLd", "sLd2", "sWq", "sWo", "sWf", "sCc", "sCg",
                   "sGm", "sGx", "sX10", "sX11",
                   "sXi0", "sXi1", "sQV0", "sQV1", "sKV0", "sKV1",
                   "sQb0", "sQb1", "sCx0", "sCx1", "sC160", "sC161",
                   "sXo0", "sXo1", "sXl0", "sXl1", "sOu0", "sOu1"):
            sem[nm] = stk.enter_context(nc.semaphore(nm))

        def sb(name, shape, dtype, st=stk):
            return st.enter_context(nc.sbuf_tensor(name, shape, dtype))

        ones32 = sb("ones32", [1, 128], F32)
        ones_bf = sb("ones_bf", [128, 128], BF16)
        ident_bf = sb("ident_bf", [128, 128], BF16)
        ident32 = sb("ident32", [128, 128], F32)
        bqrow = sb("bqrow", [1, 1536], F32)
        bqrow_bf = sb("bqrow_bf", [1, 1536], BF16)
        bsqk = sb("bsqk", [128, 8], F32)
        borow = sb("borow", [1, E], F32)
        fbrow = sb("fbrow", [1, E], F32)
        wsrow = sb("wsrow", [1, 1], F32)
        bout_bc = sb("bout_bc", [128, E], F32)
        fb_bc = sb("fb_bc", [128, E], F32)
        wsb = sb("wsb", [128, 1], F32)
        bm9 = sb("bm9", [128, 1], F32)
        m3s = sb("m3s", [128, 16], F32)
        r3s = sb("r3s", [128, 16], F32)
        aqs = sb("aqs", [128, 16], F32)
        bqs = sb("bqs", [128, 16], F32)
        gmax_a = sb("gmax_a", [128, 1], F32)
        gxall = sb("gxall", [1, 32], F32)
        gx4 = sb("gx4", [1, 4], F32)
        scale_b = sb("scale_b", [128, 1], F32)

        def Vd(eng):
            eng.drain()

        NEG_INV_E2 = -INV_E * INV_E

        def ln_coeffs(src_ap, stc, c0, sq_v, sq_ap):
            V(lambda eng: eng.tensor_reduce(stc[:, c0:c0 + 1], src_ap, AX.X, ALU.add))
            V(lambda eng, v=sq_v: eng.wait_ge(sem["sS"], v))
            V(lambda eng: eng.tensor_scalar_mul(stc[:, c0 + 1:c0 + 2],
                                                sq_ap, INV_E))
            V(Vd)
            V(lambda eng: eng.scalar_tensor_tensor(
                stc[:, c0 + 2:c0 + 3], stc[:, c0:c0 + 1], NEG_INV_E2,
                stc[:, c0:c0 + 1], ALU.mult, ALU.mult))
            V(lambda eng: eng.tensor_scalar_mul(stc[:, c0:c0 + 1],
                                                stc[:, c0:c0 + 1], INV_E))
            V(Vd)
            V(lambda eng: eng.scalar_tensor_tensor(
                stc[:, c0 + 1:c0 + 2], stc[:, c0 + 1:c0 + 2], EPS,
                stc[:, c0 + 2:c0 + 3], ALU.add, ALU.add))
            V(Vd)
            vv = inc("sV")
            V(lambda eng: eng.reciprocal(stc[:, c0 + 2:c0 + 3],
                                         stc[:, c0 + 1:c0 + 2]))
            V(lambda eng: eng.drain().then_inc(sem["sV"], 1))
            Sc(lambda eng, v=vv: eng.wait_ge(sem["sV"], v))
            sv = inc("sS")
            Sc(lambda eng: eng.activation(stc[:, c0 + 3:c0 + 4], stc[:, c0 + 2:c0 + 3],
                                          AF.Sqrt, bias=0.0, scale=1.0))
            Sc(lambda eng: eng.drain().then_inc(sem["sS"], 1))
            V(lambda eng, v=sv: eng.wait_ge(sem["sS"], v))
            rv = inc("sV")
            V(lambda eng: eng.scalar_tensor_tensor(
                stc[:, c0 + 5:c0 + 6], stc[:, c0:c0 + 1], -1.0,
                stc[:, c0 + 3:c0 + 4], ALU.mult, ALU.mult))
            V(lambda eng: eng.drain().then_inc(sem["sV"], 1))
            return rv, c0 + 3, c0 + 5

        with nc.psum_tensor("pset", [128, 512], F32) as pset:
            def ld(q, nm, dst, src):
                v = inc(nm, 16)
                q(lambda eng, dst=dst, src=src, h=sem[nm]: eng.dma_start(
                    out=dst, in_=src).then_inc(h, 16))
                return v

            ld(Y, "sLd", ones32[:, :], ones_d.ap()[0:1, :])
            ld(Y, "sLd", ident32[:, :], ident_d.ap())
            ld(Y, "sLd", bqrow[:, :], bq_d.ap())
            ld(Y, "sLd", borow[:, :], bo_d.ap())
            ld(Y, "sLd", fbrow[:, :], fb_d.ap())
            vy = ld(Y, "sLd", wsrow[:, :], ws_d.ap())
            ld(G, "sLd2", ones_bf[:, :], ones_d.ap())
            ld(G, "sLd2", ident_bf[:, :], ident_d.ap())
            vg = ld(G, "sLd2", bqrow_bf[:, :], bq_d.ap())

            V(lambda eng: eng.memset(bm9[:, :], -CSH))
            V(lambda eng: eng.memset(gmax_a[:, :], 0.0))

            T(lambda eng, v=vy: eng.wait_ge(sem["sLd"], v))
            T(lambda eng, v=vg: eng.wait_ge(sem["sLd2"], v))
            for j in range(8):
                lh, qk = j // 2, j % 2
                off = 384 * lh + 128 * qk
                tv = inc("sT")
                T(lambda eng, off=off: eng.matmul(
                    pset[:, 0:1], bqrow[0:1, off:off + 128], ones32[0:1, 0:1],
                    start=True, stop=True).then_inc(sem["sT"], 1))
                V(lambda eng, v=tv: eng.wait_ge(sem["sT"], v))
                vv = inc("sV")
                V(lambda eng, j=j: eng.tensor_copy(bsqk[:, j:j + 1], pset[:, 0:1]
                                                   ).then_inc(sem["sV"], 1))
                T(lambda eng, v=vv: eng.wait_ge(sem["sV"], v))
            for row, dst in ((borow, bout_bc), (fbrow, fb_bc)):
                for j in range(4):
                    tv = inc("sT")
                    T(lambda eng, row=row, j=j: eng.matmul(
                        pset[:, :], ones32[:, :], row[0:1, 512 * j:512 * (j + 1)],
                        start=True, stop=True).then_inc(sem["sT"], 1))
                    V(lambda eng, v=tv: eng.wait_ge(sem["sT"], v))
                    vv = inc("sV")
                    V(lambda eng, dst=dst, j=j: eng.tensor_copy(
                        dst[:, 512 * j:512 * (j + 1)], pset[:, :]
                    ).then_inc(sem["sV"], 1))
                    T(lambda eng, v=vv: eng.wait_ge(sem["sV"], v))
            tv = inc("sT")
            T(lambda eng: eng.matmul(pset[:, 0:1], ones32[:, :], wsrow[0:1, 0:1],
                                     start=True, stop=True).then_inc(sem["sT"], 1))
            V(lambda eng, v=tv: eng.wait_ge(sem["sT"], v))
            vv = inc("sV")
            V(lambda eng: eng.tensor_copy(wsb[:, :], pset[:, 0:1]
                                          ).then_inc(sem["sV"], 1))
            V(Vd)
            V(lambda eng: eng.tensor_scalar_mul(wsb[:, :], wsb[:, :], 2.0))
            m0 = barrier()
            flush_block()
        if PHASES < 1:
            return nc

        with ExitStack() as p1:
            def sb1(n, sh, dt):
                return p1.enter_context(nc.sbuf_tensor(n, sh, dt))

            xin = [sb1(f"xin{i}", [128, E], F32) for i in range(2)]
            xtm = [sb1(f"xtm{i}", [128, E], BF16) for i in range(2)]
            junk = sb1("junk", [128, E], F32)
            st1q = [sb1(f"st1q{i}", [128, 8], F32) for i in range(2)]
            xfst = [sb1(f"xfst{i}", [128, 16, 512], BF16) for i in range(2)]
            wsb1 = sb1("wqkv", [128, 16, 1536], BF16)
            qkst = [sb1(f"qkst{i}", [128, 256], BF16) for i in range(2)]
            qhm = [sb1(f"qhm{i}", [128, 2, 4, 128], BF16) for i in range(2)]
            vst = [sb1(f"vst{i}", [128, 4, 128], BF16) for i in range(2)]
            st1 = sb1("st1", [128, 8], F32)
            ptbf = [p1.enter_context(nc.psum_tensor(f"ptbf{i}", [128, 2048], BF16))
                    for i in range(2)]
            psq = [p1.enter_context(nc.psum_tensor(f"psq{i}", [128, 512], F32))
                   for i in range(2)]
            pst2 = [p1.enter_context(nc.psum_tensor(f"pst2{i}", [128, 1024], BF16))
                    for i in range(2)]

            wait_marks(G, m0)
            for ec in range(16):
                inc("sWq", 16)
                G(lambda eng, ec=ec: eng.dma_start(
                    out=wsb1[:, ec, :], in_=wq_d.ap()[128 * ec:128 * (ec + 1), :]
                ).then_inc(sem["sWq"], 16))
            wq_done = cur("sWq")

            wait_marks(Y, m0)
            wait_marks(T, m0)
            wait_marks(Sc, m0)
            wait_marks(V, m0)

            xin_free = [0, 0]
            xtm_free = [0, 0]
            xfst_free = [0, 0]
            ptbf_free = [0, 0]
            psq_free = [0, 0]
            pst2_free = [0, 0]
            qk_free = [0, 0]
            qhm_free = [0, 0]
            vst_free = [0, 0]

            for s in range(NSTRIP):
                sb_i = s % 2
                for st_ in range(4):
                    gt = s * 4 + st_
                    bi = gt % 2
                    if xin_free[bi]:
                        Y(lambda eng, v=xin_free[bi][0]: eng.wait_ge(sem["sS"], v))
                        Y(lambda eng, v=xin_free[bi][1]: eng.wait_ge(sem["sV"], v))
                    snm = f"sXi{bi}"
                    vx = inc(snm, 16)
                    Y(lambda eng, gt=gt, bi=bi, h=sem[snm]: eng.dma_start(
                        out=xin[bi][:, :], in_=xb_d.ap()[128 * gt:128 * (gt + 1), :]
                    ).then_inc(h, 16))
                    Sc(lambda eng, v=vx, h=sem[snm]: eng.wait_ge(h, v))
                    sqv = inc("sS")
                    Sc(lambda eng, bi=bi: eng.activation(
                        junk[:, :], xin[bi][:, :], AF.Square, bias=0.0, scale=1.0,
                        accum_out=st1q[bi][:, 0:1]))
                    Sc(lambda eng: eng.drain().then_inc(sem["sS"], 1))
                    V(lambda eng, v=vx, h=sem[snm]: eng.wait_ge(h, v))
                    vv, ac, bc_ = ln_coeffs(xin[bi][:, :], st1, 0, sqv,
                                            st1q[bi][:, 0:1])
                    if xtm_free[bi]:
                        V(lambda eng, v=xtm_free[bi]: eng.wait_ge(sem["sT"], v))
                    vxh = inc("sV")
                    V(lambda eng, bi=bi, ac=ac, bc_=bc_: eng.tensor_scalar(
                        xtm[bi][:, :], xin[bi][:, :], st1[:, ac:ac + 1],
                        st1[:, bc_:bc_ + 1], ALU.mult, ALU.add
                    ).then_inc(sem["sV"], 1))
                    xin_free[bi] = (sqv, vxh)
                    T(lambda eng, v=vxh: eng.wait_ge(sem["sV"], v))
                    if ptbf_free[bi]:
                        T(lambda eng, v=ptbf_free[bi]: eng.wait_ge(sem["sV"], v))
                    tv = inc("sT")
                    for j in range(16):
                        if j < 15:
                            T(lambda eng, bi=bi, j=j: eng.transpose(
                                ptbf[bi][:, 128 * j:128 * (j + 1)],
                                xtm[bi][:, 128 * j:128 * (j + 1)], ident_bf[:, :]))
                        else:
                            T(lambda eng, bi=bi, j=j: eng.transpose(
                                ptbf[bi][:, 128 * j:128 * (j + 1)],
                                xtm[bi][:, 128 * j:128 * (j + 1)], ident_bf[:, :]
                            ).then_inc(sem["sT"], 1))
                    xtm_free[bi] = tv
                    V(lambda eng, v=tv: eng.wait_ge(sem["sT"], v))
                    if st_ == 0 and xfst_free[sb_i]:
                        V(lambda eng, v=xfst_free[sb_i]: eng.wait_ge(sem["sT"], v))
                    ev = inc("sV")
                    V(lambda eng, bi=bi, sb_i=sb_i, st_=st_: eng.tensor_copy(
                        xfst[sb_i][:, :, 128 * st_:128 * (st_ + 1)],
                        ptbf[bi][:, :].rearrange("p (a b) -> p a b", b=128)
                    ).then_inc(sem["sV"], 1))
                    ptbf_free[bi] = ev
                strip_ready = cur("sV")

                if s == 0:
                    T(lambda eng, v=wq_done: eng.wait_ge(sem["sWq"], v))
                for lh in range(HPC):
                    for st_ in range(4):
                        u = (lh * 4 + st_) % 2
                        T(lambda eng, v=strip_ready: eng.wait_ge(sem["sV"], v))
                        if psq_free[u]:
                            T(lambda eng, v=psq_free[u]: eng.wait_ge(sem["sS"], v))
                        for ec in range(16):
                            T(lambda eng, u=u, sb_i=sb_i, st_=st_, ec=ec, lh=lh:
                              eng.matmul(psq[u][:, 0:384],
                                         xfst[sb_i][:, ec, 128 * st_:128 * (st_ + 1)],
                                         wsb1[:, ec, 384 * lh:384 * (lh + 1)],
                                         start=(ec == 0), stop=False))
                        tqv = inc("sT")
                        T(lambda eng, u=u, lh=lh: eng.matmul(
                            psq[u][:, 256:384], ones_bf[0:1, :],
                            bqrow_bf[0:1, 384 * lh + 256:384 * lh + 384],
                            start=False, stop=True).then_inc(sem["sT"], 1))
                        if lh == HPC - 1 and st_ == 3:
                            xfst_free[sb_i] = tqv
                        Sc(lambda eng, v=tqv: eng.wait_ge(sem["sT"], v))
                        if qk_free[u]:
                            Sc(lambda eng, v=qk_free[u]: eng.wait_ge(sem["sT"], v))
                        Sc(lambda eng, u=u: eng.copy(qkst[u][:, :], psq[u][:, 0:256]))
                        if st_ == 0 and vst_free[sb_i]:
                            Sc(lambda eng, v=vst_free[sb_i], h=sem[f"sQV{sb_i}"]:
                               eng.wait_ge(h, v))
                        sev = inc("sS")
                        Sc(lambda eng, u=u, sb_i=sb_i, st_=st_: eng.copy(
                            vst[sb_i][:, st_, :], psq[u][:, 256:384]
                        ).then_inc(sem["sS"], 1))
                        psq_free[u] = sev
                        T(lambda eng, v=sev: eng.wait_ge(sem["sS"], v))
                        if pst2_free[u]:
                            T(lambda eng, v=pst2_free[u]: eng.wait_ge(sem["sS"], v))
                        T(lambda eng, u=u: eng.transpose(
                            pst2[u][:, 0:128], qkst[u][:, 0:128], ident_bf[:, :]))
                        ttv = inc("sT")
                        T(lambda eng, u=u: eng.transpose(
                            pst2[u][:, 128:256], qkst[u][:, 128:256], ident_bf[:, :]
                        ).then_inc(sem["sT"], 1))
                        qk_free[u] = ttv
                        Sc(lambda eng, v=ttv: eng.wait_ge(sem["sT"], v))
                        if st_ == 0 and qhm_free[sb_i]:
                            Sc(lambda eng, v=qhm_free[sb_i], h=sem[f"sQV{sb_i}"]:
                               eng.wait_ge(h, v))
                        Sc(lambda eng, u=u, sb_i=sb_i, st_=st_, lh=lh: eng.activation(
                            qhm[sb_i][:, 0, st_, :], pst2[u][:, 0:128], AF.Identity,
                            bias=bsqk[:, 2 * lh:2 * lh + 1], scale=1.0))
                        sqv2 = inc("sS")
                        Sc(lambda eng, u=u, sb_i=sb_i, st_=st_, lh=lh: eng.activation(
                            qhm[sb_i][:, 1, st_, :], pst2[u][:, 128:256], AF.Identity,
                            bias=bsqk[:, 2 * lh + 1:2 * lh + 2], scale=1.0
                        ).then_inc(sem["sS"], 1))
                        pst2_free[u] = sqv2
                    Y(lambda eng, v=cur("sS"): eng.wait_ge(sem["sS"], v))
                    hQV = sem[f"sQV{sb_i}"]
                    inc(f"sQV{sb_i}", 16)
                    Y(lambda eng, lh=lh, s=s, sb_i=sb_i, h=hQV: eng.dma_start(
                        out=qc_d.ap()[lh, :, 512 * s:512 * (s + 1)],
                        in_=qhm[sb_i][:, 0, :, :].rearrange("p a b -> p (a b)")
                    ).then_inc(h, 16))
                    inc(f"sQV{sb_i}", 16)
                    Y(lambda eng, lh=lh, s=s, sb_i=sb_i, h=hQV: eng.dma_start(
                        out=kc_d.ap()[lh, :, 512 * s:512 * (s + 1)],
                        in_=qhm[sb_i][:, 1, :, :].rearrange("p a b -> p (a b)")
                    ).then_inc(h, 16))
                    vv_ = inc(f"sQV{sb_i}", 16)
                    Y(lambda eng, lh=lh, s=s, sb_i=sb_i, h=hQV: eng.dma_start(
                        out=vc_d.ap()[lh, 512 * s:512 * (s + 1), :].rearrange(
                            "(a p) d -> p a d", p=128),
                        in_=vst[sb_i][:, :, :]).then_inc(h, 16))
                    qhm_free[sb_i] = vv_
                    vst_free[sb_i] = vv_
            ph1_st = (cur("sQV0"), cur("sQV1"))
            m1 = barrier()
            flush_block()
        if PHASES < 2:
            return nc

        wout_scope = ExitStack()
        Wout = wout_scope.enter_context(nc.sbuf_tensor("Wout", [128, 16, E], BF16))
        with ExitStack() as p2:
            def sb2(n, sh, dt):
                return p2.enter_context(nc.sbuf_tensor(n, sh, dt))

            Ksb = [sb2(f"Ksb{i}", [128, S], BF16) for i in range(2)]
            Vsb = [sb2(f"Vsb{i}", [128, 64, 128], BF16) for i in range(2)]
            Qsb = [sb2(f"Qsb{i}", [128, 512], BF16) for i in range(2)]
            pTt = sb2("pTt", [128, 4, 512], BF16)
            ladd = sb2("ladd", [128, 512], F32)
            lbf = sb2("lbf", [128, 512], BF16)
            lrec = sb2("lrec", [1, 512], BF16)
            lb = sb2("lb", [128, 512], BF16)
            ctxe = [sb2(f"ctxe{i}", [128, 512], BF16) for i in range(2)]

            psc = p2.enter_context(nc.psum_tensor("psc", [128, 2048], F32))
            pctx = [p2.enter_context(nc.psum_tensor(f"pctx{i}", [128, 512], F32))
                    for i in range(2)]
            pmis = p2.enter_context(nc.psum_tensor("pmis", [128, 512], F32))

            wait_marks(G, m1)
            for gH in range(16):
                inc("sWo", 16)
                G(lambda eng, gH=gH: eng.dma_start(
                    out=Wout[:, gH, :], in_=wo_d.ap()[128 * gH:128 * (gH + 1), :]
                ).then_inc(sem["sWo"], 16))
            wout_done = cur("sWo")

            wait_marks(Y, m1)
            Y(lambda eng, v=ph1_st[0]: eng.wait_ge(sem["sQV0"], v))
            Y(lambda eng, v=ph1_st[1]: eng.wait_ge(sem["sQV1"], v))
            wait_marks(T, m1)
            wait_marks(Sc, m1)
            wait_marks(V, m1)

            kv_loaded = [0, 0]
            q_loaded = [0, 0]
            head_done = [0, 0]
            qb_done = [0, 0]
            exp_done = {}
            tinc_sc = {}
            tinc_cx = {}
            ladd_done = {}
            ctxe_free = [0, 0]
            pctx_free = [0, 0]
            pmis_free = 0
            lrec_cons = 0
            lbf_cons = 0
            lb_cons = 0

            for lh in range(min(HPC, PH2_HEADS)):
                hb = lh % 2
                if head_done[hb]:
                    Y(lambda eng, v=head_done[hb]: eng.wait_ge(sem["sT"], v))
                hKV = sem[f"sKV{hb}"]
                inc(f"sKV{hb}", 16)
                Y(lambda eng, lh=lh, hb=hb, h=hKV: eng.dma_start(
                    out=Ksb[hb][:, :], in_=kc_d.ap()[lh]).then_inc(h, 16))
                for i4 in range(16):
                    inc(f"sKV{hb}", 16)
                    Y(lambda eng, lh=lh, hb=hb, i4=i4, h=hKV: eng.dma_start(
                        out=Vsb[hb][:, 4 * i4:4 * (i4 + 1), :],
                        in_=vc_d.ap()[lh].rearrange("(a p) d -> p a d", p=128)[
                            :, 4 * i4:4 * (i4 + 1), :]).then_inc(h, 16))
                kv_loaded[hb] = cur(f"sKV{hb}")

                for qb in range(min(16, PH2_QB)):
                    qbb = qb % 2
                    un = lh * 16 + qb
                    if qb_done[qbb]:
                        Y(lambda eng, v=qb_done[qbb]: eng.wait_ge(sem["sT"], v))
                    hQb = sem[f"sQb{qbb}"]
                    inc(f"sQb{qbb}", 16)
                    Y(lambda eng, lh=lh, qb=qb, qbb=qbb, h=hQb: eng.dma_start(
                        out=Qsb[qbb][:, :],
                        in_=qc_d.ap()[lh, :, 512 * qb:512 * (qb + 1)]
                    ).then_inc(h, 16))
                    q_loaded[qbb] = cur(f"sQb{qbb}")

                    T(lambda eng, v=kv_loaded[hb], h=sem[f"sKV{hb}"]: eng.wait_ge(h, v))
                    T(lambda eng, v=q_loaded[qbb], h=hQb: eng.wait_ge(h, v))
                    V(lambda eng: eng.memset(ladd[:, :], 0.0))

                    for g in range(NG):
                        gi = un * NG + g
                        if gi >= 2:
                            T(lambda eng, v=exp_done[gi - 2]: eng.wait_ge(sem["sS"], v))
                        for j in range(2):
                            kt = 2 * g + j
                            bank = (g % 2) * 2 + j
                            if j == 0:
                                T(lambda eng, hb=hb, qbb=qbb, kt=kt, bank=bank:
                                  eng.matmul(psc[:, 512 * bank:512 * (bank + 1)],
                                             Ksb[hb][:, 128 * kt:128 * (kt + 1)],
                                             Qsb[qbb][:, :], start=True, stop=True))
                            else:
                                T(lambda eng, hb=hb, qbb=qbb, kt=kt, bank=bank:
                                  eng.matmul(psc[:, 512 * bank:512 * (bank + 1)],
                                             Ksb[hb][:, 128 * kt:128 * (kt + 1)],
                                             Qsb[qbb][:, :], start=True, stop=True
                                             ).then_inc(sem["sT"], 1))
                        tinc_sc[gi] = inc("sT")
                        if g >= 1:
                            gp = gi - 1
                            T(lambda eng, v=exp_done[gp]: eng.wait_ge(sem["sS"], v))
                            if g == 1 and pctx_free[un % 2]:
                                T(lambda eng, v=pctx_free[un % 2]:
                                  eng.wait_ge(sem["sV"], v))
                            for j in range(2):
                                kt = 2 * (g - 1) + j
                                if j == 0:
                                    T(lambda eng, hb=hb, un=un, g=g, kt=kt, j=j:
                                      eng.matmul(pctx[un % 2][:, :],
                                                 Vsb[hb][:, kt, :],
                                                 pTt[:, 2 * ((g - 1) % 2) + j, :],
                                                 start=(kt == 0), stop=False))
                                else:
                                    T(lambda eng, hb=hb, un=un, g=g, kt=kt, j=j:
                                      eng.matmul(pctx[un % 2][:, :],
                                                 Vsb[hb][:, kt, :],
                                                 pTt[:, 2 * ((g - 1) % 2) + j, :],
                                                 start=False, stop=False
                                                 ).then_inc(sem["sT"], 1))
                            tinc_cx[gp] = inc("sT")
                        Sc(lambda eng, v=tinc_sc[gi]: eng.wait_ge(sem["sT"], v))
                        if gi >= 2:
                            Sc(lambda eng, v=tinc_cx[gi - 2]: eng.wait_ge(sem["sT"], v))
                            Sc(lambda eng, v=ladd_done[gi - 2]: eng.wait_ge(sem["sV"], v))
                        exp_done[gi] = inc("sS")
                        Sc(lambda eng, g=g: eng.activation(
                            pTt[:, 2 * (g % 2):2 * (g % 2) + 2, :].rearrange(
                                "p a b -> p (a b)"),
                            psc[:, 1024 * (g % 2):1024 * (g % 2) + 1024],
                            AF.Exp, bias=bm9[:, :], scale=SCL).then_inc(sem["sS"], 1))
                        V(lambda eng, v=exp_done[gi]: eng.wait_ge(sem["sS"], v))
                        V(lambda eng, g=g: eng.tensor_add(
                            ladd[:, :], ladd[:, :], pTt[:, 2 * (g % 2), :]))
                        ladd_done[gi] = inc("sV")
                        V(lambda eng, g=g: eng.tensor_add(
                            ladd[:, :], ladd[:, :], pTt[:, 2 * (g % 2) + 1, :]
                        ).then_inc(sem["sV"], 1))
                    glast = un * NG + NG - 1
                    T(lambda eng, v=exp_done[glast]: eng.wait_ge(sem["sS"], v))
                    for j in range(2):
                        kt = 2 * (NG - 1) + j
                        if j == 0:
                            T(lambda eng, hb=hb, un=un, kt=kt, j=j: eng.matmul(
                                pctx[un % 2][:, :], Vsb[hb][:, kt, :],
                                pTt[:, 2 * ((NG - 1) % 2) + j, :],
                                start=False, stop=False))
                        else:
                            T(lambda eng, hb=hb, un=un, kt=kt, j=j: eng.matmul(
                                pctx[un % 2][:, :], Vsb[hb][:, kt, :],
                                pTt[:, 2 * ((NG - 1) % 2) + j, :],
                                start=False, stop=True).then_inc(sem["sT"], 1))
                    tinc_cx[glast] = inc("sT")
                    if qb == 15:
                        head_done[hb] = tinc_cx[glast]
                    qb_done[qbb] = tinc_cx[glast]

                    if lbf_cons:
                        V(lambda eng, v=lbf_cons: eng.wait_ge(sem["sT"], v))
                    vlb = inc("sV")
                    V(lambda eng: eng.tensor_copy(lbf[:, :], ladd[:, :]
                                                  ).then_inc(sem["sV"], 1))
                    T(lambda eng, v=vlb: eng.wait_ge(sem["sV"], v))
                    if pmis_free:
                        T(lambda eng, v=pmis_free: eng.wait_ge(sem["sS"], v))
                    lbf_cons = inc("sT")
                    T(lambda eng: eng.matmul(pmis[0:1, :], ones_bf[:, 0:1], lbf[:, :],
                                             start=True, stop=True
                                             ).then_inc(sem["sT"], 1))
                    V(lambda eng, v=lbf_cons: eng.wait_ge(sem["sT"], v))
                    if lrec_cons:
                        V(lambda eng, v=lrec_cons: eng.wait_ge(sem["sT"], v))
                    vrec = inc("sV")

                    def _recip(eng):
                        with nc.allow_low_precision(
                                reason="softmax 1/l stored bf16; within bar"):
                            eng.reciprocal(lrec[:, :], pmis[0:1, :])
                    V(_recip)
                    V(lambda eng: eng.drain().then_inc(sem["sV"], 1))
                    T(lambda eng, v=vrec: eng.wait_ge(sem["sV"], v))
                    lrec_cons = inc("sT")
                    T(lambda eng: eng.matmul(pmis[:, :], ones_bf[0:1, :],
                                             lrec[0:1, :], start=True, stop=True
                                             ).then_inc(sem["sT"], 1))
                    Sc(lambda eng, v=lrec_cons: eng.wait_ge(sem["sT"], v))
                    if lb_cons:
                        Sc(lambda eng, v=lb_cons: eng.wait_ge(sem["sV"], v))
                    pmis_free = inc("sS")
                    Sc(lambda eng: eng.copy(lb[:, :], pmis[:, :]
                                            ).then_inc(sem["sS"], 1))
                    V(lambda eng, v=pmis_free: eng.wait_ge(sem["sS"], v))
                    if ctxe_free[un % 2]:
                        V(lambda eng, v=ctxe_free[un % 2], h=sem[f"sCx{un % 2}"]:
                          eng.wait_ge(h, v))
                    lb_cons = inc("sV")
                    V(lambda eng, un=un: eng.tensor_tensor(
                        ctxe[un % 2][:, :], pctx[un % 2][:, :], lb[:, :], ALU.mult
                    ).then_inc(sem["sV"], 1))
                    pctx_free[un % 2] = lb_cons
                    Y(lambda eng, v=lb_cons: eng.wait_ge(sem["sV"], v))
                    vst_ = inc(f"sCx{un % 2}", 16)
                    Y(lambda eng, lh=lh, qb=qb, un=un, h=sem[f"sCx{un % 2}"]:
                      eng.dma_start(
                          out=ctxc_d.ap()[lh, :, 512 * qb:512 * (qb + 1)],
                          in_=ctxe[un % 2][:, :]).then_inc(h, 16))
                    ctxe_free[un % 2] = vst_
                if PH2_CC:
                    G(lambda eng, v=cur("sCx0"): eng.wait_ge(sem["sCx0"], v))
                    G(lambda eng, v=cur("sCx1"): eng.wait_ge(sem["sCx1"], v))
                    inc("sCc")
                    G(lambda eng, lh=lh: eng.collective_compute(
                        "AllGather", ALU.bypass,
                        replica_groups=[[0, 1, 2, 3], [4, 5, 6, 7]],
                        ins=[ctxc_d.ap()[lh]], outs=[ctxg_d.ap()[lh]],
                    ).then_inc(sem["sCc"], 1))
            cc_heads = cur("sCc")
            m2 = barrier()
            flush_block()
        if PHASES < 3:
            wout_scope.close()
            return nc

        with ExitStack() as p3:
            def sb3(n, sh, dt):
                return p3.enter_context(nc.sbuf_tensor(n, sh, dt))

            ctx16 = [sb3(f"ctx16{i}", [128, 16, 512], BF16) for i in range(2)]
            xot = [sb3(f"xot{i}", [128, E], F32) for i in range(2)]
            ao = sb3("ao", [128, E], F32)
            x2h = sb3("x2h", [128, E], F32)
            x1t = [sb3(f"x1t{i}", [128, E], F32) for i in range(2)]
            xh3 = sb3("xh3", [128, E], F32)
            junk3 = sb3("junk3", [128, E], F32)
            st3 = sb3("st3", [128, 16], F32)
            st3qa = sb3("st3qa", [128, 8], F32)
            st3qb = sb3("st3qb", [128, 8], F32)
            pso = [p3.enter_context(nc.psum_tensor(f"pso{i}", [128, 2048], F32))
                   for i in range(2)]

            wait_marks(Y, m2)
            Y(lambda eng, v=cc_heads: eng.wait_ge(sem["sCc"], v))
            Y(lambda eng, v=wout_done: eng.wait_ge(sem["sWo"], v))
            def _mkq(eng):
                q0 = eng.partition_id() % 4
                rt["qidx"] = [eng.compute_val(q0 * 4 + uu) for uu in range(4)]
            Y(_mkq)
            wait_marks(T, m2)
            T(lambda eng, v=wout_done: eng.wait_ge(sem["sWo"], v))
            wait_marks(Sc, m2)
            wait_marks(V, m2)

            ctx_ld = [0, 0]
            ctx16_free = [0, 0]
            x1_free = [0, 0]
            pso_free = [0, 0]
            ao_free = 0
            x1t_sc = [0, 0]
            tile_tov = {}
            for t in range(16):
                u, tb = t // 4, t % 2
                ub = (t // 4) % 2
                if t % 4 == 0:
                    if ctx16_free[ub]:
                        Y(lambda eng, v=ctx16_free[ub]: eng.wait_ge(sem["sT"], v))
                    for i4 in range(4):
                        inc(f"sC16{ub}", 16)
                        Y(lambda eng, i4=i4, u=u, ub=ub, h=sem[f"sC16{ub}"]:
                          eng.dma_start(
                              out=ctx16[ub][:, 4 * i4:4 * (i4 + 1), :],
                              in_=ctxg_d.ap()[:, i4, :, ts(rt["qidx"][u], 512)
                                              ].transpose([1, 0, 2])
                          ).then_inc(h, 16))
                    ctx_ld[ub] = cur(f"sC16{ub}")
                if x1_free[tb]:
                    Y(lambda eng, v=x1_free[tb]: eng.wait_ge(sem["sV"], v))
                inc(f"sXo{tb}", 16)
                Y(lambda eng, t=t, tb=tb, h=sem[f"sXo{tb}"]: eng.dma_start(
                    out=xot[tb][:, :], in_=xo_d.ap()[128 * t:128 * (t + 1), :]
                ).then_inc(h, 16))
                xv = cur(f"sXo{tb}")

                T(lambda eng, v=ctx_ld[ub], h=sem[f"sC16{ub}"]: eng.wait_ge(h, v))
                if pso_free[t % 2]:
                    T(lambda eng, v=pso_free[t % 2]: eng.wait_ge(sem["sV"], v))
                for gH in range(16):
                    for j in range(4):
                        if gH == 15 and j == 3:
                            T(lambda eng, t=t, gH=gH, j=j, ub=ub: eng.matmul(
                                pso[t % 2][:, 512 * j:512 * (j + 1)],
                                ctx16[ub][:, gH, 128 * (t % 4):128 * (t % 4) + 128],
                                Wout[:, gH, 512 * j:512 * (j + 1)],
                                start=False, stop=True).then_inc(sem["sT"], 1))
                        else:
                            T(lambda eng, t=t, gH=gH, j=j, ub=ub: eng.matmul(
                                pso[t % 2][:, 512 * j:512 * (j + 1)],
                                ctx16[ub][:, gH, 128 * (t % 4):128 * (t % 4) + 128],
                                Wout[:, gH, 512 * j:512 * (j + 1)],
                                start=(gH == 0), stop=(gH == 15)))
                tov = inc("sT")
                tile_tov[t] = tov
                if t % 4 == 3:
                    ctx16_free[ub] = tov
                V(lambda eng, v=tov: eng.wait_ge(sem["sT"], v))
                if ao_free:
                    V(lambda eng, v=ao_free: eng.wait_ge(sem["sS"], v))
                vao = inc("sV")
                V(lambda eng, t=t: eng.tensor_add(ao[:, :], pso[t % 2][:, :],
                                                  bout_bc[:, :]).then_inc(sem["sV"], 1))
                pso_free[t % 2] = vao
                Sc(lambda eng, v=vao: eng.wait_ge(sem["sV"], v))
                sq2 = inc("sS")
                Sc(lambda eng: eng.activation(junk3[:, :], ao[:, :], AF.Square,
                                              bias=0.0, scale=1.0,
                                              accum_out=st3qa[:, 0:1]))
                Sc(lambda eng: eng.drain().then_inc(sem["sS"], 1))
                V(lambda eng, v=vao: eng.wait_ge(sem["sV"], v))
                vv2, ac2, bc2 = ln_coeffs(ao[:, :], st3, 0, sq2, st3qa[:, 0:1])
                ao_free = sq2
                V(lambda eng, ac2=ac2, bc2=bc2: eng.tensor_scalar(
                    x2h[:, :], ao[:, :], st3[:, ac2:ac2 + 1],
                    st3[:, bc2:bc2 + 1], ALU.mult, ALU.add))
                V(lambda eng, v=xv, h=sem[f"sXo{tb}"]: eng.wait_ge(h, v))
                if x1t_sc[tb]:
                    V(lambda eng, v=x1t_sc[tb]: eng.wait_ge(sem["sS"], v))
                if t >= 2:
                    V(lambda eng, v=cur(f"sX1{tb}"), h=sem[f"sX1{tb}"]:
                      eng.wait_ge(h, v))
                vx1 = inc("sV")
                V(lambda eng, tb=tb: eng.tensor_add(x1t[tb][:, :], xot[tb][:, :],
                                                    x2h[:, :]).then_inc(sem["sV"], 1))
                Y(lambda eng, v=vx1: eng.wait_ge(sem["sV"], v))
                inc(f"sX1{tb}", 16)
                Y(lambda eng, t=t, tb=tb, h=sem[f"sX1{tb}"]: eng.dma_start(
                    out=x1_d.ap()[128 * t:128 * (t + 1), :], in_=x1t[tb][:, :]
                ).then_inc(h, 16))
                Sc(lambda eng, v=vx1: eng.wait_ge(sem["sV"], v))
                sq3 = inc("sS")
                Sc(lambda eng, tb=tb: eng.activation(junk3[:, :], x1t[tb][:, :],
                                                     AF.Square, bias=0.0, scale=1.0,
                                                     accum_out=st3qb[:, 0:1]))
                Sc(lambda eng: eng.drain().then_inc(sem["sS"], 1))
                vv3, ac3, bc3 = ln_coeffs(x1t[tb][:, :], st3, 8, sq3,
                                          st3qb[:, 0:1])
                x1t_sc[tb] = sq3
                V(lambda eng, t=t: eng.tensor_copy(m3s[:, t:t + 1], st3[:, 8:9]))
                V(lambda eng, t=t, ac3=ac3: eng.tensor_copy(
                    r3s[:, t:t + 1], st3[:, ac3:ac3 + 1]))
                V(lambda eng, tb=tb, ac3=ac3, bc3=bc3: eng.tensor_scalar(
                    xh3[:, :], x1t[tb][:, :], st3[:, ac3:ac3 + 1],
                    st3[:, bc3:bc3 + 1], ALU.mult, ALU.add))
                V(lambda eng: eng.tensor_reduce(st3[:, 14:15], xh3[:, :], AX.X,
                                                ALU.max, apply_absolute_value=True))
                V(Vd)
                x1_free[tb] = inc("sV")
                V(lambda eng: eng.tensor_max(gmax_a[:, :], gmax_a[:, :],
                                             st3[:, 14:15]).then_inc(sem["sV"], 1))
            m3m = barrier()
            flush_block()
        if PHASES < 4:
            wout_scope.close()
            return nc

        with nc.psum_tensor("ptl", [128, 512], F32) as ptl:
            wait_marks(T, m3m)
            ttl = inc("sT")
            T(lambda eng: eng.transpose(ptl[0:1, 0:128], gmax_a[:, :],
                                        ident32[:, :]).then_inc(sem["sT"], 1))
            wait_marks(V, m3m)
            V(lambda eng, v=ttl: eng.wait_ge(sem["sT"], v))
            V(lambda eng: eng.tensor_reduce(gx4[:, 0:1], ptl[0:1, 0:128], AX.X,
                                            ALU.max))
            V(Vd)
            V(lambda eng: eng.tensor_copy(gx4[:, 1:2], gx4[:, 0:1]))
            V(lambda eng: eng.tensor_copy(gx4[:, 2:3], gx4[:, 0:1]))
            vg4 = inc("sV")
            V(lambda eng: eng.tensor_copy(gx4[:, 3:4], gx4[:, 0:1]))
            V(lambda eng: eng.drain().then_inc(sem["sV"], 1))
            wait_marks(G, m3m)
            G(lambda eng, v=vg4: eng.wait_ge(sem["sV"], v))
            vgi = inc("sGm", 16)
            G(lambda eng: eng.dma_start(out=gmi_d.ap(), in_=gx4[:, :]
                                        ).then_inc(sem["sGm"], 16))
            G(lambda eng, v=vgi: eng.wait_ge(sem["sGm"], v))
            vcc = inc("sCg")
            G(lambda eng: eng.collective_compute(
                "AllGather", ALU.bypass,
                replica_groups=[[0, 1, 2, 3, 4, 5, 6, 7]],
                ins=[gmi_d.ap()], outs=[gmo_d.ap()]).then_inc(sem["sCg"], 1))
            wait_marks(Y, m3m)
            Y(lambda eng, v=vcc: eng.wait_ge(sem["sCg"], v))
            vgl = inc("sGx", 16)
            Y(lambda eng: eng.dma_start(
                out=gxall[:, :], in_=gmo_d.ap().rearrange("a b c -> b (a c)")
            ).then_inc(sem["sGx"], 16))
            V(lambda eng, v=vgl: eng.wait_ge(sem["sGx"], v))
            gx_s0 = gxall[:, :].rearrange("a (c s) -> a s c", s=4)[:, 0, :]
            V(lambda eng: eng.tensor_reduce(gx4[:, 0:1], gx_s0, AX.X, ALU.max))
            V(Vd)
            V(lambda eng: eng.tensor_scalar(gx4[:, 0:1], gx4[:, 0:1], 1.0 / 448.0,
                                            1e-7, ALU.mult, ALU.max))
            V(Vd)
            V(lambda eng: eng.reciprocal(gx4[:, 1:2], gx4[:, 0:1]))
            V(Vd)
            V(lambda eng: eng.tensor_tensor(gx4[:, 2:3], gx4[:, 0:1], gx4[:, 1:2],
                                            ALU.mult))
            V(Vd)
            V(lambda eng: eng.tensor_scalar(gx4[:, 2:3], gx4[:, 2:3], -1.0, 2.0,
                                            ALU.mult, ALU.add))
            V(Vd)
            V(lambda eng: eng.tensor_tensor(gx4[:, 1:2], gx4[:, 1:2], gx4[:, 2:3],
                                            ALU.mult))
            V(Vd)
            vsc = inc("sV")
            V(lambda eng: eng.tensor_scalar_mul(gx4[:, 1:2], gx4[:, 1:2], 0.5))
            V(lambda eng: eng.drain().then_inc(sem["sV"], 1))
            T(lambda eng, v=vsc: eng.wait_ge(sem["sV"], v))
            tsc = inc("sT")
            T(lambda eng: eng.matmul(ptl[:, 0:1], ones32[:, :], gx4[0:1, 1:2],
                                     start=True, stop=True).then_inc(sem["sT"], 1))
            V(lambda eng, v=tsc: eng.wait_ge(sem["sT"], v))
            V(lambda eng: eng.tensor_copy(scale_b[:, :], ptl[:, 0:1]))
            V(Vd)
            for t in range(16):
                V(lambda eng, t=t: eng.tensor_tensor(aqs[:, t:t + 1], r3s[:, t:t + 1],
                                                     scale_b[:, :], ALU.mult))
            V(Vd)
            for t in range(16):
                V(lambda eng, t=t: eng.scalar_tensor_tensor(
                    bqs[:, t:t + 1], m3s[:, t:t + 1], -1.0, aqs[:, t:t + 1],
                    ALU.mult, ALU.mult))
            inc("sV")
            V(lambda eng: eng.drain().then_inc(sem["sV"], 1))
            m3t = barrier()
            flush_block()
        wout_scope.close()
        if PHASES < 5:
            return nc

        with ExitStack() as p4:
            def sb4(n, sh, dt):
                return p4.enter_context(nc.sbuf_tensor(n, sh, dt))

            wT = sb4("wT", [128, 16, E], BF16)
            x1i = [sb4(f"x1i{i}", [128, E], F32) for i in range(2)]
            xqt = sb4("xqt", [128, E], F32)
            xq8 = sb4("xq8", [128, E], FP8)
            xqbf = sb4("xqbf", [128, E], BF16)
            xqf = sb4("xqf", [128, 16, 128], BF16)
            ffo = sb4("ffo", [128, E], F32)
            x4h = sb4("x4h", [128, E], F32)
            junk4 = sb4("junk4", [128, E], F32)
            st4 = sb4("st4", [128, 8], F32)
            st4q = sb4("st4q", [128, 8], F32)
            outt = [sb4(f"outt{i}", [128, E], F32) for i in range(2)]
            psf = p4.enter_context(nc.psum_tensor("psf", [128, 2048], F32))
            ptb4 = [p4.enter_context(nc.psum_tensor(f"ptb4{i}", [128, 1024], BF16))
                    for i in range(2)]

            wait_marks(G, m3t)
            for ic in range(16):
                inc("sWf", 16)
                G(lambda eng, ic=ic: eng.dma_start(
                    out=wT[:, ic, :], in_=wt_d.ap()[128 * ic:128 * (ic + 1), :]
                ).then_inc(sem["sWf"], 16))
            wt_done = cur("sWf")

            wait_marks(Y, m3t)
            Y(lambda eng, v=cur("sX10"): eng.wait_ge(sem["sX10"], v))
            Y(lambda eng, v=cur("sX11"): eng.wait_ge(sem["sX11"], v))
            wait_marks(T, m3t)
            wait_marks(Sc, m3t)
            wait_marks(V, m3t)
            T(lambda eng, v=wt_done: eng.wait_ge(sem["sWf"], v))

            x1_ld = [0, 0]
            out_free = [0, 0]
            psf_free = 0
            ptb_free = [0, 0]
            xq_free = 0
            xqt_free = 0
            ffo_free = 0
            for t in range(16):
                tb = t % 2
                if x1_ld[tb]:
                    Y(lambda eng, v=x1_ld[tb]: eng.wait_ge(sem["sV"], v))
                inc(f"sXl{tb}", 16)
                Y(lambda eng, t=t, tb=tb, h=sem[f"sXl{tb}"]: eng.dma_start(
                    out=x1i[tb][:, :], in_=x1_d.ap()[128 * t:128 * (t + 1), :]
                ).then_inc(h, 16))
                xv = cur(f"sXl{tb}")
                V(lambda eng, v=xv, h=sem[f"sXl{tb}"]: eng.wait_ge(h, v))
                if xqt_free:
                    V(lambda eng, v=xqt_free: eng.wait_ge(sem["sS"], v))
                tsv = inc("sV")
                V(lambda eng, t=t, tb=tb: eng.tensor_scalar(
                    xqt[:, :], x1i[tb][:, :], aqs[:, t:t + 1], bqs[:, t:t + 1],
                    ALU.mult, ALU.add).then_inc(sem["sV"], 1))
                Sc(lambda eng, v=tsv: eng.wait_ge(sem["sV"], v))
                xqt_free = inc("sS")
                Sc(lambda eng: eng.copy(xq8[:, :], xqt[:, :]).then_inc(sem["sS"], 1))
                if xq_free:
                    Sc(lambda eng, v=xq_free: eng.wait_ge(sem["sT"], v))
                sqv = inc("sS")
                Sc(lambda eng: eng.copy(xqbf[:, :], xq8[:, :]).then_inc(sem["sS"], 1))
                T(lambda eng, v=sqv: eng.wait_ge(sem["sS"], v))
                for half in range(2):
                    if ptb_free[half]:
                        T(lambda eng, v=ptb_free[half]: eng.wait_ge(sem["sV"], v))
                    tv = inc("sT")
                    for j in range(8):
                        jj = half * 8 + j
                        if j < 7:
                            T(lambda eng, half=half, j=j, jj=jj: eng.transpose(
                                ptb4[half][:, 128 * j:128 * (j + 1)],
                                xqbf[:, 128 * jj:128 * (jj + 1)], ident_bf[:, :]))
                        else:
                            T(lambda eng, half=half, j=j, jj=jj: eng.transpose(
                                ptb4[half][:, 128 * j:128 * (j + 1)],
                                xqbf[:, 128 * jj:128 * (jj + 1)], ident_bf[:, :]
                            ).then_inc(sem["sT"], 1))
                    V(lambda eng, v=tv: eng.wait_ge(sem["sT"], v))
                    pv = inc("sV")
                    V(lambda eng, half=half: eng.tensor_copy(
                        xqf[:, 8 * half:8 * (half + 1), :].rearrange("p a b -> p (a b)"),
                        ptb4[half][:, :]).then_inc(sem["sV"], 1))
                    ptb_free[half] = pv
                xq_free = cur("sT")
                vxq = cur("sV")
                T(lambda eng, v=vxq: eng.wait_ge(sem["sV"], v))
                if psf_free:
                    T(lambda eng, v=psf_free: eng.wait_ge(sem["sV"], v))
                for ic in range(16):
                    for j in range(4):
                        if ic == 15 and j == 3:
                            T(lambda eng, ic=ic, j=j: eng.matmul(
                                psf[:, 512 * j:512 * (j + 1)], xqf[:, ic, :],
                                wT[:, ic, 512 * j:512 * (j + 1)],
                                start=False, stop=True).then_inc(sem["sT"], 1))
                        else:
                            T(lambda eng, ic=ic, j=j: eng.matmul(
                                psf[:, 512 * j:512 * (j + 1)], xqf[:, ic, :],
                                wT[:, ic, 512 * j:512 * (j + 1)],
                                start=(ic == 0), stop=(ic == 15)))
                tf = inc("sT")
                V(lambda eng, v=tf: eng.wait_ge(sem["sT"], v))
                if ffo_free:
                    V(lambda eng, v=ffo_free: eng.wait_ge(sem["sS"], v))
                psf_free = inc("sV")
                V(lambda eng: eng.scalar_tensor_tensor(
                    ffo[:, :], psf[:, :], wsb[:, :], fb_bc[:, :], ALU.mult, ALU.add
                ).then_inc(sem["sV"], 1))
                Sc(lambda eng, v=psf_free: eng.wait_ge(sem["sV"], v))
                sq4 = inc("sS")
                Sc(lambda eng: eng.activation(junk4[:, :], ffo[:, :], AF.Square,
                                              bias=0.0, scale=1.0,
                                              accum_out=st4q[:, 0:1]))
                Sc(lambda eng: eng.drain().then_inc(sem["sS"], 1))
                vv4, ac4, bc4 = ln_coeffs(ffo[:, :], st4, 0, sq4, st4q[:, 0:1])
                ffo_free = sq4
                V(lambda eng, ac4=ac4, bc4=bc4: eng.tensor_scalar(
                    x4h[:, :], ffo[:, :], st4[:, ac4:ac4 + 1],
                    st4[:, bc4:bc4 + 1], ALU.mult, ALU.add))
                if out_free[tb]:
                    V(lambda eng, v=out_free[tb], h=sem[f"sOu{tb}"]:
                      eng.wait_ge(h, v))
                x1_ld[tb] = inc("sV")
                V(lambda eng, tb=tb: eng.tensor_add(outt[tb][:, :], x1i[tb][:, :],
                                                    x4h[:, :]).then_inc(sem["sV"], 1))
                Y(lambda eng, v=x1_ld[tb]: eng.wait_ge(sem["sV"], v))
                ov = inc(f"sOu{tb}", 16)
                Y(lambda eng, t=t, tb=tb, h=sem[f"sOu{tb}"]: eng.dma_start(
                    out=out_d.ap()[128 * t:128 * (t + 1), :], in_=outt[tb][:, :]
                ).then_inc(h, 16))
                out_free[tb] = ov
            Y(lambda eng, v=cur("sOu0"): eng.wait_ge(sem["sOu0"], v))
            Y(lambda eng, v=cur("sOu1"): eng.wait_ge(sem["sOu1"], v))
            flush_block()

    return nc


def kernel(L0_attn_x, L0_attn_W_qkv, L0_attn_b_qkv, L0_attn_W_out, L0_attn_b_out,
           L0_ffn_weight, L0_ffn_weight_scale, L0_ffn_bias):
    x = np.ascontiguousarray(np.asarray(L0_attn_x, dtype=np.float32))
    wq = np.ascontiguousarray(np.asarray(L0_attn_W_qkv, dtype=np.float32))
    bq = np.asarray(L0_attn_b_qkv, dtype=np.float32)
    wo = np.ascontiguousarray(np.asarray(L0_attn_W_out, dtype=np.float32))
    bo = np.ascontiguousarray(np.asarray(L0_attn_b_out, dtype=np.float32)
                              .reshape(1, E))
    wf = np.asarray(L0_ffn_weight, dtype=np.float32)
    ws = np.ascontiguousarray(np.asarray(L0_ffn_weight_scale, dtype=np.float32)
                              .reshape(1, 1))
    fb = np.ascontiguousarray(np.asarray(L0_ffn_bias, dtype=np.float32)
                              .reshape(1, E))
    wt = np.ascontiguousarray(wf.T)

    in_maps = []
    for c in range(8):
        b, g = c // 4, c % 4
        in_maps.append({
            "xb": x[b],
            "xo": np.ascontiguousarray(x[b, TOK * g:TOK * (g + 1)]),
            "wq": np.ascontiguousarray(wq[:, 1536 * g:1536 * (g + 1)]),
            "bq": np.ascontiguousarray(bq[1536 * g:1536 * (g + 1)].reshape(1, 1536)),
            "wo": wo,
            "bo": bo,
            "wt": wt,
            "fb": fb,
            "ws": ws,
        })
    nc = build()
    res = run_bass_kernel_spmd(nc, in_maps, core_ids=list(range(8)))
    out = np.stack([np.asarray(res.results[c]["out"], dtype=np.float32)
                    for c in range(8)])
    return out.reshape(B, S, E)


# version 11
# speedup vs baseline: 1.0392x; 1.0007x over previous
import math
import numpy as np
import concourse.bass as bass
import concourse.mybir as mybir
from concourse.bass import ts
from concourse.bass_utils import run_bass_kernel_spmd
from contextlib import ExitStack

F32 = mybir.dt.float32
BF16 = mybir.dt.bfloat16
FP8 = mybir.dt.float8e4
AF = mybir.ActivationFunctionType
ALU = mybir.AluOpType
AX = mybir.AxisListType

B, S, E, H, DH = 2, 8192, 2048, 16, 128
HPC = 4
TOK = 2048
NSTRIP = 16
CSH = 9.0
SCL = float(np.float32(1.0) / np.float32(math.sqrt(DH)))
EPS = 1e-5
INV_E = float(np.float32(1.0 / E))
NG = 32
PHASES = 99
PH2_HEADS = 4
PH2_QB = 16
PH2_CC = True


def build():
    nc = bass.Bass("TRN2", target_bir_lowering=False,
                   detect_race_conditions=False)

    xb_d = nc.dram_tensor("xb", [S, E], F32, kind="ExternalInput")
    xo_d = nc.dram_tensor("xo", [TOK, E], F32, kind="ExternalInput")
    wq_d = nc.dram_tensor("wq", [E, 1536], F32, kind="ExternalInput")
    bq_d = nc.dram_tensor("bq", [1, 1536], F32, kind="ExternalInput")
    wo_d = nc.dram_tensor("wo", [E, E], F32, kind="ExternalInput")
    bo_d = nc.dram_tensor("bo", [1, E], F32, kind="ExternalInput")
    wt_d = nc.dram_tensor("wt", [E, E], F32, kind="ExternalInput")
    fb_d = nc.dram_tensor("fb", [1, E], F32, kind="ExternalInput")
    ws_d = nc.dram_tensor("ws", [1, 1], F32, kind="ExternalInput")
    out_d = nc.dram_tensor("out", [TOK, E], F32, kind="ExternalOutput")

    qc_d = nc.dram_tensor("qc", [HPC, DH, S], BF16)
    kc_d = nc.dram_tensor("kc", [HPC, DH, S], BF16)
    vc_d = nc.dram_tensor("vc", [HPC, S, DH], BF16)
    ctxc_d = nc.dram_tensor("ctxc", [HPC, DH, S], BF16)
    ctxg_d = nc.dram_tensor("ctxg", [HPC, 4, DH, S], BF16)
    x1_d = nc.dram_tensor("x1s", [TOK, E], F32)
    gmi_d = nc.dram_tensor("gmi", [1, 4], F32)
    gmo_d = nc.dram_tensor("gmo", [8, 1, 4], F32)

    ident_d = nc.inline_tensor(np.eye(128, dtype=np.float32), name="identc")
    ones_d = nc.inline_tensor(np.ones((128, 128), dtype=np.float32), name="onesc")

    ops = {k: [] for k in "TSVYG"}
    T = ops["T"].append
    Sc = ops["S"].append
    V = ops["V"].append
    Y = ops["Y"].append
    G = ops["G"].append

    cnt = {}
    sem = {}

    def inc(name, n=1):
        cnt[name] = cnt.get(name, 0) + n
        return cnt[name]

    def cur(name):
        return cnt.get(name, 0)

    def flush_block():
        local = {k: list(v) for k, v in ops.items()}
        for k in ops:
            ops[k].clear()
        with nc.Block() as blk:
            @blk.tensor
            def _(eng):
                for f in local["T"]:
                    f(eng)

            @blk.scalar
            def _(eng):
                for f in local["S"]:
                    f(eng)

            @blk.vector
            def _(eng):
                for f in local["V"]:
                    f(eng)

            @blk.sync
            def _(eng):
                for f in local["Y"]:
                    f(eng)

            @blk.gpsimd
            def _(eng):
                for f in local["G"]:
                    f(eng)

    def barrier():
        marks = {}
        for add, nm in ((T, "sTb"), (Sc, "sSb"), (V, "sVb")):
            v = inc(nm)
            add(lambda eng, h=sem[nm]: eng.drain().then_inc(h, 1))
            marks[nm] = v
        return marks

    def wait_marks(add, marks):
        for nm, v in marks.items():
            add(lambda eng, h=sem[nm], v=v: eng.wait_ge(h, v))

    rt = {}

    with ExitStack() as stk:
        for nm in ("sT", "sS", "sV", "sTb", "sSb", "sVb",
                   "sLd", "sLd2", "sWq", "sWo", "sWf", "sCc", "sCg",
                   "sGm", "sGx", "sX10", "sX11",
                   "sXi0", "sXi1", "sQV0", "sQV1", "sKV0", "sKV1",
                   "sQb0", "sQb1", "sCx0", "sCx1", "sC160", "sC161",
                   "sXo0", "sXo1", "sXl0", "sXl1", "sOu0", "sOu1"):
            sem[nm] = stk.enter_context(nc.semaphore(nm))

        def sb(name, shape, dtype, st=stk):
            return st.enter_context(nc.sbuf_tensor(name, shape, dtype))

        ones32 = sb("ones32", [1, 128], F32)
        ones_bf = sb("ones_bf", [128, 128], BF16)
        ident_bf = sb("ident_bf", [128, 128], BF16)
        ident32 = sb("ident32", [128, 128], F32)
        bqrow = sb("bqrow", [1, 1536], F32)
        bqrow_bf = sb("bqrow_bf", [1, 1536], BF16)
        bsqk = sb("bsqk", [128, 8], F32)
        borow = sb("borow", [1, E], F32)
        fbrow = sb("fbrow", [1, E], F32)
        wsrow = sb("wsrow", [1, 1], F32)
        bout_bc = sb("bout_bc", [128, E], F32)
        fb_bc = sb("fb_bc", [128, E], F32)
        wsb = sb("wsb", [128, 1], F32)
        bm9 = sb("bm9", [128, 1], F32)
        m3s = sb("m3s", [128, 16], F32)
        r3s = sb("r3s", [128, 16], F32)
        aqs = sb("aqs", [128, 16], F32)
        bqs = sb("bqs", [128, 16], F32)
        gmax_a = sb("gmax_a", [128, 1], F32)
        gxall = sb("gxall", [1, 32], F32)
        gx4 = sb("gx4", [1, 4], F32)
        scale_b = sb("scale_b", [128, 1], F32)

        def Vd(eng):
            eng.drain()

        NEG_INV_E2 = -INV_E * INV_E

        def ln_coeffs(src_ap, stc, c0, sq_v, sq_ap):
            V(lambda eng: eng.tensor_reduce(stc[:, c0:c0 + 1], src_ap, AX.X, ALU.add))
            V(lambda eng, v=sq_v: eng.wait_ge(sem["sS"], v))
            V(lambda eng: eng.tensor_scalar_mul(stc[:, c0 + 1:c0 + 2],
                                                sq_ap, INV_E))
            V(Vd)
            V(lambda eng: eng.scalar_tensor_tensor(
                stc[:, c0 + 2:c0 + 3], stc[:, c0:c0 + 1], NEG_INV_E2,
                stc[:, c0:c0 + 1], ALU.mult, ALU.mult))
            V(lambda eng: eng.tensor_scalar_mul(stc[:, c0:c0 + 1],
                                                stc[:, c0:c0 + 1], INV_E))
            V(Vd)
            V(lambda eng: eng.scalar_tensor_tensor(
                stc[:, c0 + 1:c0 + 2], stc[:, c0 + 1:c0 + 2], EPS,
                stc[:, c0 + 2:c0 + 3], ALU.add, ALU.add))
            V(Vd)
            vv = inc("sV")
            V(lambda eng: eng.reciprocal(stc[:, c0 + 2:c0 + 3],
                                         stc[:, c0 + 1:c0 + 2]))
            V(lambda eng: eng.drain().then_inc(sem["sV"], 1))
            Sc(lambda eng, v=vv: eng.wait_ge(sem["sV"], v))
            sv = inc("sS")
            Sc(lambda eng: eng.activation(stc[:, c0 + 3:c0 + 4], stc[:, c0 + 2:c0 + 3],
                                          AF.Sqrt, bias=0.0, scale=1.0))
            Sc(lambda eng: eng.drain().then_inc(sem["sS"], 1))
            V(lambda eng, v=sv: eng.wait_ge(sem["sS"], v))
            rv = inc("sV")
            V(lambda eng: eng.scalar_tensor_tensor(
                stc[:, c0 + 5:c0 + 6], stc[:, c0:c0 + 1], -1.0,
                stc[:, c0 + 3:c0 + 4], ALU.mult, ALU.mult))
            V(lambda eng: eng.drain().then_inc(sem["sV"], 1))
            return rv, c0 + 3, c0 + 5

        with nc.psum_tensor("pset", [128, 512], F32) as pset:
            def ld(q, nm, dst, src):
                v = inc(nm, 16)
                q(lambda eng, dst=dst, src=src, h=sem[nm]: eng.dma_start(
                    out=dst, in_=src).then_inc(h, 16))
                return v

            ld(Y, "sLd", ones32[:, :], ones_d.ap()[0:1, :])
            ld(Y, "sLd", ident32[:, :], ident_d.ap())
            ld(Y, "sLd", bqrow[:, :], bq_d.ap())
            ld(Y, "sLd", borow[:, :], bo_d.ap())
            ld(Y, "sLd", fbrow[:, :], fb_d.ap())
            vy = ld(Y, "sLd", wsrow[:, :], ws_d.ap())
            ld(G, "sLd2", ones_bf[:, :], ones_d.ap())
            ld(G, "sLd2", ident_bf[:, :], ident_d.ap())
            vg = ld(G, "sLd2", bqrow_bf[:, :], bq_d.ap())

            V(lambda eng: eng.memset(bm9[:, :], -CSH))
            V(lambda eng: eng.memset(gmax_a[:, :], 0.0))

            T(lambda eng, v=vy: eng.wait_ge(sem["sLd"], v))
            T(lambda eng, v=vg: eng.wait_ge(sem["sLd2"], v))
            for j in range(8):
                lh, qk = j // 2, j % 2
                off = 384 * lh + 128 * qk
                tv = inc("sT")
                T(lambda eng, off=off: eng.matmul(
                    pset[:, 0:1], bqrow[0:1, off:off + 128], ones32[0:1, 0:1],
                    start=True, stop=True).then_inc(sem["sT"], 1))
                V(lambda eng, v=tv: eng.wait_ge(sem["sT"], v))
                vv = inc("sV")
                V(lambda eng, j=j: eng.tensor_copy(bsqk[:, j:j + 1], pset[:, 0:1]
                                                   ).then_inc(sem["sV"], 1))
                T(lambda eng, v=vv: eng.wait_ge(sem["sV"], v))
            for row, dst in ((borow, bout_bc), (fbrow, fb_bc)):
                for j in range(4):
                    tv = inc("sT")
                    T(lambda eng, row=row, j=j: eng.matmul(
                        pset[:, :], ones32[:, :], row[0:1, 512 * j:512 * (j + 1)],
                        start=True, stop=True).then_inc(sem["sT"], 1))
                    V(lambda eng, v=tv: eng.wait_ge(sem["sT"], v))
                    vv = inc("sV")
                    V(lambda eng, dst=dst, j=j: eng.tensor_copy(
                        dst[:, 512 * j:512 * (j + 1)], pset[:, :]
                    ).then_inc(sem["sV"], 1))
                    T(lambda eng, v=vv: eng.wait_ge(sem["sV"], v))
            tv = inc("sT")
            T(lambda eng: eng.matmul(pset[:, 0:1], ones32[:, :], wsrow[0:1, 0:1],
                                     start=True, stop=True).then_inc(sem["sT"], 1))
            V(lambda eng, v=tv: eng.wait_ge(sem["sT"], v))
            vv = inc("sV")
            V(lambda eng: eng.tensor_copy(wsb[:, :], pset[:, 0:1]
                                          ).then_inc(sem["sV"], 1))
            V(Vd)
            V(lambda eng: eng.tensor_scalar_mul(wsb[:, :], wsb[:, :], 2.0))
            m0 = barrier()
            flush_block()
        if PHASES < 1:
            return nc

        with ExitStack() as p1:
            def sb1(n, sh, dt):
                return p1.enter_context(nc.sbuf_tensor(n, sh, dt))

            xin = [sb1(f"xin{i}", [128, E], F32) for i in range(2)]
            xtm = [sb1(f"xtm{i}", [128, E], BF16) for i in range(2)]
            junk = sb1("junk", [128, E], F32)
            st1q = [sb1(f"st1q{i}", [128, 8], F32) for i in range(2)]
            xfst = [sb1(f"xfst{i}", [128, 16, 512], BF16) for i in range(2)]
            wsb1 = sb1("wqkv", [128, 16, 1536], BF16)
            qkst = [sb1(f"qkst{i}", [128, 256], BF16) for i in range(2)]
            qhm = [sb1(f"qhm{i}", [128, 2, 4, 128], BF16) for i in range(2)]
            vst = [sb1(f"vst{i}", [128, 4, 128], BF16) for i in range(2)]
            st1 = sb1("st1", [128, 8], F32)
            ptbf = [p1.enter_context(nc.psum_tensor(f"ptbf{i}", [128, 2048], BF16))
                    for i in range(2)]
            psq = [p1.enter_context(nc.psum_tensor(f"psq{i}", [128, 512], F32))
                   for i in range(2)]
            pst2 = [p1.enter_context(nc.psum_tensor(f"pst2{i}", [128, 1024], BF16))
                    for i in range(2)]

            wait_marks(G, m0)
            for ec in range(16):
                inc("sWq", 16)
                G(lambda eng, ec=ec: eng.dma_start(
                    out=wsb1[:, ec, :], in_=wq_d.ap()[128 * ec:128 * (ec + 1), :]
                ).then_inc(sem["sWq"], 16))
            wq_done = cur("sWq")

            wait_marks(Y, m0)
            wait_marks(T, m0)
            wait_marks(Sc, m0)
            wait_marks(V, m0)

            xin_free = [0, 0]
            xtm_free = [0, 0]
            xfst_free = [0, 0]
            ptbf_free = [0, 0]
            psq_free = [0, 0]
            pst2_free = [0, 0]
            qk_free = [0, 0]
            qhm_free = [0, 0]
            vst_free = [0, 0]

            for s in range(NSTRIP):
                sb_i = s % 2
                for st_ in range(4):
                    gt = s * 4 + st_
                    bi = gt % 2
                    if xin_free[bi]:
                        Y(lambda eng, v=xin_free[bi][0]: eng.wait_ge(sem["sS"], v))
                        Y(lambda eng, v=xin_free[bi][1]: eng.wait_ge(sem["sV"], v))
                    snm = f"sXi{bi}"
                    vx = inc(snm, 16)
                    Y(lambda eng, gt=gt, bi=bi, h=sem[snm]: eng.dma_start(
                        out=xin[bi][:, :], in_=xb_d.ap()[128 * gt:128 * (gt + 1), :]
                    ).then_inc(h, 16))
                    Sc(lambda eng, v=vx, h=sem[snm]: eng.wait_ge(h, v))
                    sqv = inc("sS")
                    Sc(lambda eng, bi=bi: eng.activation(
                        junk[:, :], xin[bi][:, :], AF.Square, bias=0.0, scale=1.0,
                        accum_out=st1q[bi][:, 0:1]))
                    Sc(lambda eng: eng.drain().then_inc(sem["sS"], 1))
                    V(lambda eng, v=vx, h=sem[snm]: eng.wait_ge(h, v))
                    vv, ac, bc_ = ln_coeffs(xin[bi][:, :], st1, 0, sqv,
                                            st1q[bi][:, 0:1])
                    if xtm_free[bi]:
                        V(lambda eng, v=xtm_free[bi]: eng.wait_ge(sem["sT"], v))
                    vxh = inc("sV")
                    V(lambda eng, bi=bi, ac=ac, bc_=bc_: eng.tensor_scalar(
                        xtm[bi][:, :], xin[bi][:, :], st1[:, ac:ac + 1],
                        st1[:, bc_:bc_ + 1], ALU.mult, ALU.add
                    ).then_inc(sem["sV"], 1))
                    xin_free[bi] = (sqv, vxh)
                    T(lambda eng, v=vxh: eng.wait_ge(sem["sV"], v))
                    if ptbf_free[bi]:
                        T(lambda eng, v=ptbf_free[bi]: eng.wait_ge(sem["sV"], v))
                    tv = inc("sT")
                    for j in range(16):
                        if j < 15:
                            T(lambda eng, bi=bi, j=j: eng.transpose(
                                ptbf[bi][:, 128 * j:128 * (j + 1)],
                                xtm[bi][:, 128 * j:128 * (j + 1)], ident_bf[:, :]))
                        else:
                            T(lambda eng, bi=bi, j=j: eng.transpose(
                                ptbf[bi][:, 128 * j:128 * (j + 1)],
                                xtm[bi][:, 128 * j:128 * (j + 1)], ident_bf[:, :]
                            ).then_inc(sem["sT"], 1))
                    xtm_free[bi] = tv
                    V(lambda eng, v=tv: eng.wait_ge(sem["sT"], v))
                    if st_ == 0 and xfst_free[sb_i]:
                        V(lambda eng, v=xfst_free[sb_i]: eng.wait_ge(sem["sT"], v))
                    ev = inc("sV")
                    V(lambda eng, bi=bi, sb_i=sb_i, st_=st_: eng.tensor_copy(
                        xfst[sb_i][:, :, 128 * st_:128 * (st_ + 1)],
                        ptbf[bi][:, :].rearrange("p (a b) -> p a b", b=128)
                    ).then_inc(sem["sV"], 1))
                    ptbf_free[bi] = ev
                strip_ready = cur("sV")

                if s == 0:
                    T(lambda eng, v=wq_done: eng.wait_ge(sem["sWq"], v))
                for lh in range(HPC):
                    for st_ in range(4):
                        u = (lh * 4 + st_) % 2
                        T(lambda eng, v=strip_ready: eng.wait_ge(sem["sV"], v))
                        if psq_free[u]:
                            T(lambda eng, v=psq_free[u]: eng.wait_ge(sem["sS"], v))
                        for ec in range(16):
                            T(lambda eng, u=u, sb_i=sb_i, st_=st_, ec=ec, lh=lh:
                              eng.matmul(psq[u][:, 0:384],
                                         xfst[sb_i][:, ec, 128 * st_:128 * (st_ + 1)],
                                         wsb1[:, ec, 384 * lh:384 * (lh + 1)],
                                         start=(ec == 0), stop=False))
                        tqv = inc("sT")
                        T(lambda eng, u=u, lh=lh: eng.matmul(
                            psq[u][:, 256:384], ones_bf[0:1, :],
                            bqrow_bf[0:1, 384 * lh + 256:384 * lh + 384],
                            start=False, stop=True).then_inc(sem["sT"], 1))
                        if lh == HPC - 1 and st_ == 3:
                            xfst_free[sb_i] = tqv
                        Sc(lambda eng, v=tqv: eng.wait_ge(sem["sT"], v))
                        if qk_free[u]:
                            Sc(lambda eng, v=qk_free[u]: eng.wait_ge(sem["sT"], v))
                        Sc(lambda eng, u=u: eng.copy(qkst[u][:, :], psq[u][:, 0:256]))
                        if st_ == 0 and vst_free[sb_i]:
                            Sc(lambda eng, v=vst_free[sb_i], h=sem[f"sQV{sb_i}"]:
                               eng.wait_ge(h, v))
                        sev = inc("sS")
                        Sc(lambda eng, u=u, sb_i=sb_i, st_=st_: eng.copy(
                            vst[sb_i][:, st_, :], psq[u][:, 256:384]
                        ).then_inc(sem["sS"], 1))
                        psq_free[u] = sev
                        T(lambda eng, v=sev: eng.wait_ge(sem["sS"], v))
                        if pst2_free[u]:
                            T(lambda eng, v=pst2_free[u]: eng.wait_ge(sem["sS"], v))
                        T(lambda eng, u=u: eng.transpose(
                            pst2[u][:, 0:128], qkst[u][:, 0:128], ident_bf[:, :]))
                        ttv = inc("sT")
                        T(lambda eng, u=u: eng.transpose(
                            pst2[u][:, 128:256], qkst[u][:, 128:256], ident_bf[:, :]
                        ).then_inc(sem["sT"], 1))
                        qk_free[u] = ttv
                        Sc(lambda eng, v=ttv: eng.wait_ge(sem["sT"], v))
                        if st_ == 0 and qhm_free[sb_i]:
                            Sc(lambda eng, v=qhm_free[sb_i], h=sem[f"sQV{sb_i}"]:
                               eng.wait_ge(h, v))
                        Sc(lambda eng, u=u, sb_i=sb_i, st_=st_, lh=lh: eng.activation(
                            qhm[sb_i][:, 0, st_, :], pst2[u][:, 0:128], AF.Identity,
                            bias=bsqk[:, 2 * lh:2 * lh + 1], scale=1.0))
                        sqv2 = inc("sS")
                        Sc(lambda eng, u=u, sb_i=sb_i, st_=st_, lh=lh: eng.activation(
                            qhm[sb_i][:, 1, st_, :], pst2[u][:, 128:256], AF.Identity,
                            bias=bsqk[:, 2 * lh + 1:2 * lh + 2], scale=1.0
                        ).then_inc(sem["sS"], 1))
                        pst2_free[u] = sqv2
                    Y(lambda eng, v=cur("sS"): eng.wait_ge(sem["sS"], v))
                    hQV = sem[f"sQV{sb_i}"]
                    inc(f"sQV{sb_i}", 16)
                    Y(lambda eng, lh=lh, s=s, sb_i=sb_i, h=hQV: eng.dma_start(
                        out=qc_d.ap()[lh, :, 512 * s:512 * (s + 1)],
                        in_=qhm[sb_i][:, 0, :, :].rearrange("p a b -> p (a b)")
                    ).then_inc(h, 16))
                    inc(f"sQV{sb_i}", 16)
                    Y(lambda eng, lh=lh, s=s, sb_i=sb_i, h=hQV: eng.dma_start(
                        out=kc_d.ap()[lh, :, 512 * s:512 * (s + 1)],
                        in_=qhm[sb_i][:, 1, :, :].rearrange("p a b -> p (a b)")
                    ).then_inc(h, 16))
                    vv_ = inc(f"sQV{sb_i}", 16)
                    Y(lambda eng, lh=lh, s=s, sb_i=sb_i, h=hQV: eng.dma_start(
                        out=vc_d.ap()[lh, 512 * s:512 * (s + 1), :].rearrange(
                            "(a p) d -> p a d", p=128),
                        in_=vst[sb_i][:, :, :]).then_inc(h, 16))
                    qhm_free[sb_i] = vv_
                    vst_free[sb_i] = vv_
            ph1_st = (cur("sQV0"), cur("sQV1"))
            m1 = barrier()
            flush_block()
        if PHASES < 2:
            return nc

        wout_scope = ExitStack()
        Wout = wout_scope.enter_context(nc.sbuf_tensor("Wout", [128, 16, E], BF16))
        with ExitStack() as p2:
            def sb2(n, sh, dt):
                return p2.enter_context(nc.sbuf_tensor(n, sh, dt))

            Ksb = [sb2(f"Ksb{i}", [128, S], BF16) for i in range(2)]
            Vsb = [sb2(f"Vsb{i}", [128, 64, 128], BF16) for i in range(2)]
            Qsb = [sb2(f"Qsb{i}", [128, 512], BF16) for i in range(2)]
            pTt = sb2("pTt", [128, 4, 512], BF16)
            ladd = sb2("ladd", [128, 1024], F32)
            lbf = sb2("lbf", [128, 512], BF16)
            lrec = sb2("lrec", [1, 512], BF16)
            lb = sb2("lb", [128, 512], BF16)
            ctxe = [sb2(f"ctxe{i}", [128, 512], BF16) for i in range(2)]

            psc = p2.enter_context(nc.psum_tensor("psc", [128, 2048], F32))
            pctx = [p2.enter_context(nc.psum_tensor(f"pctx{i}", [128, 512], F32))
                    for i in range(2)]
            pmis = p2.enter_context(nc.psum_tensor("pmis", [128, 512], F32))

            wait_marks(G, m1)
            for gH in range(16):
                inc("sWo", 16)
                G(lambda eng, gH=gH: eng.dma_start(
                    out=Wout[:, gH, :], in_=wo_d.ap()[128 * gH:128 * (gH + 1), :]
                ).then_inc(sem["sWo"], 16))
            wout_done = cur("sWo")

            wait_marks(Y, m1)
            Y(lambda eng, v=ph1_st[0]: eng.wait_ge(sem["sQV0"], v))
            Y(lambda eng, v=ph1_st[1]: eng.wait_ge(sem["sQV1"], v))
            wait_marks(T, m1)
            wait_marks(Sc, m1)
            wait_marks(V, m1)

            kv_loaded = [0, 0]
            q_loaded = [0, 0]
            head_done = [0, 0]
            qb_done = [0, 0]
            exp_done = {}
            tinc_sc = {}
            tinc_cx = {}
            ladd_done = {}
            ctxe_free = [0, 0]
            pctx_free = [0, 0]
            pmis_free = 0
            lrec_cons = 0
            lbf_cons = 0
            lb_cons = 0

            for lh in range(min(HPC, PH2_HEADS)):
                hb = lh % 2
                if head_done[hb]:
                    Y(lambda eng, v=head_done[hb]: eng.wait_ge(sem["sT"], v))
                hKV = sem[f"sKV{hb}"]
                inc(f"sKV{hb}", 16)
                Y(lambda eng, lh=lh, hb=hb, h=hKV: eng.dma_start(
                    out=Ksb[hb][:, :], in_=kc_d.ap()[lh]).then_inc(h, 16))
                for i4 in range(16):
                    inc(f"sKV{hb}", 16)
                    Y(lambda eng, lh=lh, hb=hb, i4=i4, h=hKV: eng.dma_start(
                        out=Vsb[hb][:, 4 * i4:4 * (i4 + 1), :],
                        in_=vc_d.ap()[lh].rearrange("(a p) d -> p a d", p=128)[
                            :, 4 * i4:4 * (i4 + 1), :]).then_inc(h, 16))
                kv_loaded[hb] = cur(f"sKV{hb}")

                for qb in range(min(16, PH2_QB)):
                    qbb = qb % 2
                    un = lh * 16 + qb
                    if qb_done[qbb]:
                        Y(lambda eng, v=qb_done[qbb]: eng.wait_ge(sem["sT"], v))
                    hQb = sem[f"sQb{qbb}"]
                    inc(f"sQb{qbb}", 16)
                    Y(lambda eng, lh=lh, qb=qb, qbb=qbb, h=hQb: eng.dma_start(
                        out=Qsb[qbb][:, :],
                        in_=qc_d.ap()[lh, :, 512 * qb:512 * (qb + 1)]
                    ).then_inc(h, 16))
                    q_loaded[qbb] = cur(f"sQb{qbb}")

                    T(lambda eng, v=kv_loaded[hb], h=sem[f"sKV{hb}"]: eng.wait_ge(h, v))
                    T(lambda eng, v=q_loaded[qbb], h=hQb: eng.wait_ge(h, v))
                    V(lambda eng: eng.memset(ladd[:, :], 0.0))

                    for g in range(NG):
                        gi = un * NG + g
                        if gi >= 2:
                            T(lambda eng, v=exp_done[gi - 2]: eng.wait_ge(sem["sS"], v))
                        for j in range(2):
                            kt = 2 * g + j
                            bank = (g % 2) * 2 + j
                            if j == 0:
                                T(lambda eng, hb=hb, qbb=qbb, kt=kt, bank=bank:
                                  eng.matmul(psc[:, 512 * bank:512 * (bank + 1)],
                                             Ksb[hb][:, 128 * kt:128 * (kt + 1)],
                                             Qsb[qbb][:, :], start=True, stop=True))
                            else:
                                T(lambda eng, hb=hb, qbb=qbb, kt=kt, bank=bank:
                                  eng.matmul(psc[:, 512 * bank:512 * (bank + 1)],
                                             Ksb[hb][:, 128 * kt:128 * (kt + 1)],
                                             Qsb[qbb][:, :], start=True, stop=True
                                             ).then_inc(sem["sT"], 1))
                        tinc_sc[gi] = inc("sT")
                        if g >= 1:
                            gp = gi - 1
                            T(lambda eng, v=exp_done[gp]: eng.wait_ge(sem["sS"], v))
                            if g == 1 and pctx_free[un % 2]:
                                T(lambda eng, v=pctx_free[un % 2]:
                                  eng.wait_ge(sem["sV"], v))
                            for j in range(2):
                                kt = 2 * (g - 1) + j
                                if j == 0:
                                    T(lambda eng, hb=hb, un=un, g=g, kt=kt, j=j:
                                      eng.matmul(pctx[un % 2][:, :],
                                                 Vsb[hb][:, kt, :],
                                                 pTt[:, 2 * ((g - 1) % 2) + j, :],
                                                 start=(kt == 0), stop=False))
                                else:
                                    T(lambda eng, hb=hb, un=un, g=g, kt=kt, j=j:
                                      eng.matmul(pctx[un % 2][:, :],
                                                 Vsb[hb][:, kt, :],
                                                 pTt[:, 2 * ((g - 1) % 2) + j, :],
                                                 start=False, stop=False
                                                 ).then_inc(sem["sT"], 1))
                            tinc_cx[gp] = inc("sT")
                        Sc(lambda eng, v=tinc_sc[gi]: eng.wait_ge(sem["sT"], v))
                        if gi >= 2:
                            Sc(lambda eng, v=tinc_cx[gi - 2]: eng.wait_ge(sem["sT"], v))
                            Sc(lambda eng, v=ladd_done[gi - 2]: eng.wait_ge(sem["sV"], v))
                        exp_done[gi] = inc("sS")
                        Sc(lambda eng, g=g: eng.activation(
                            pTt[:, 2 * (g % 2):2 * (g % 2) + 2, :].rearrange(
                                "p a b -> p (a b)"),
                            psc[:, 1024 * (g % 2):1024 * (g % 2) + 1024],
                            AF.Exp, bias=bm9[:, :], scale=SCL).then_inc(sem["sS"], 1))
                        V(lambda eng, v=exp_done[gi]: eng.wait_ge(sem["sS"], v))
                        ladd_done[gi] = inc("sV")
                        V(lambda eng, g=g: eng.tensor_add(
                            ladd[:, :], ladd[:, :],
                            pTt[:, 2 * (g % 2):2 * (g % 2) + 2, :].rearrange(
                                "p a b -> p (a b)")).then_inc(sem["sV"], 1))
                    glast = un * NG + NG - 1
                    T(lambda eng, v=exp_done[glast]: eng.wait_ge(sem["sS"], v))
                    for j in range(2):
                        kt = 2 * (NG - 1) + j
                        if j == 0:
                            T(lambda eng, hb=hb, un=un, kt=kt, j=j: eng.matmul(
                                pctx[un % 2][:, :], Vsb[hb][:, kt, :],
                                pTt[:, 2 * ((NG - 1) % 2) + j, :],
                                start=False, stop=False))
                        else:
                            T(lambda eng, hb=hb, un=un, kt=kt, j=j: eng.matmul(
                                pctx[un % 2][:, :], Vsb[hb][:, kt, :],
                                pTt[:, 2 * ((NG - 1) % 2) + j, :],
                                start=False, stop=True).then_inc(sem["sT"], 1))
                    tinc_cx[glast] = inc("sT")
                    if qb == 15:
                        head_done[hb] = tinc_cx[glast]
                    qb_done[qbb] = tinc_cx[glast]

                    if lbf_cons:
                        V(lambda eng, v=lbf_cons: eng.wait_ge(sem["sT"], v))
                    vlb = inc("sV")
                    V(lambda eng: eng.tensor_tensor(lbf[:, :], ladd[:, 0:512],
                                                    ladd[:, 512:1024], ALU.add
                                                    ).then_inc(sem["sV"], 1))
                    T(lambda eng, v=vlb: eng.wait_ge(sem["sV"], v))
                    if pmis_free:
                        T(lambda eng, v=pmis_free: eng.wait_ge(sem["sS"], v))
                    lbf_cons = inc("sT")
                    T(lambda eng: eng.matmul(pmis[0:1, :], ones_bf[:, 0:1], lbf[:, :],
                                             start=True, stop=True
                                             ).then_inc(sem["sT"], 1))
                    V(lambda eng, v=lbf_cons: eng.wait_ge(sem["sT"], v))
                    if lrec_cons:
                        V(lambda eng, v=lrec_cons: eng.wait_ge(sem["sT"], v))
                    vrec = inc("sV")

                    def _recip(eng):
                        with nc.allow_low_precision(
                                reason="softmax 1/l stored bf16; within bar"):
                            eng.reciprocal(lrec[:, :], pmis[0:1, :])
                    V(_recip)
                    V(lambda eng: eng.drain().then_inc(sem["sV"], 1))
                    T(lambda eng, v=vrec: eng.wait_ge(sem["sV"], v))
                    lrec_cons = inc("sT")
                    T(lambda eng: eng.matmul(pmis[:, :], ones_bf[0:1, :],
                                             lrec[0:1, :], start=True, stop=True
                                             ).then_inc(sem["sT"], 1))
                    Sc(lambda eng, v=lrec_cons: eng.wait_ge(sem["sT"], v))
                    if lb_cons:
                        Sc(lambda eng, v=lb_cons: eng.wait_ge(sem["sV"], v))
                    pmis_free = inc("sS")
                    Sc(lambda eng: eng.copy(lb[:, :], pmis[:, :]
                                            ).then_inc(sem["sS"], 1))
                    V(lambda eng, v=pmis_free: eng.wait_ge(sem["sS"], v))
                    if ctxe_free[un % 2]:
                        V(lambda eng, v=ctxe_free[un % 2], h=sem[f"sCx{un % 2}"]:
                          eng.wait_ge(h, v))
                    lb_cons = inc("sV")
                    V(lambda eng, un=un: eng.tensor_tensor(
                        ctxe[un % 2][:, :], pctx[un % 2][:, :], lb[:, :], ALU.mult
                    ).then_inc(sem["sV"], 1))
                    pctx_free[un % 2] = lb_cons
                    Y(lambda eng, v=lb_cons: eng.wait_ge(sem["sV"], v))
                    vst_ = inc(f"sCx{un % 2}", 16)
                    Y(lambda eng, lh=lh, qb=qb, un=un, h=sem[f"sCx{un % 2}"]:
                      eng.dma_start(
                          out=ctxc_d.ap()[lh, :, 512 * qb:512 * (qb + 1)],
                          in_=ctxe[un % 2][:, :]).then_inc(h, 16))
                    ctxe_free[un % 2] = vst_
                if PH2_CC:
                    G(lambda eng, v=cur("sCx0"): eng.wait_ge(sem["sCx0"], v))
                    G(lambda eng, v=cur("sCx1"): eng.wait_ge(sem["sCx1"], v))
                    inc("sCc")
                    G(lambda eng, lh=lh: eng.collective_compute(
                        "AllGather", ALU.bypass,
                        replica_groups=[[0, 1, 2, 3], [4, 5, 6, 7]],
                        ins=[ctxc_d.ap()[lh]], outs=[ctxg_d.ap()[lh]],
                    ).then_inc(sem["sCc"], 1))
            cc_heads = cur("sCc")
            m2 = barrier()
            flush_block()
        if PHASES < 3:
            wout_scope.close()
            return nc

        with ExitStack() as p3:
            def sb3(n, sh, dt):
                return p3.enter_context(nc.sbuf_tensor(n, sh, dt))

            ctx16 = [sb3(f"ctx16{i}", [128, 16, 512], BF16) for i in range(2)]
            xot = [sb3(f"xot{i}", [128, E], F32) for i in range(2)]
            ao = sb3("ao", [128, E], F32)
            x2h = sb3("x2h", [128, E], F32)
            x1t = [sb3(f"x1t{i}", [128, E], F32) for i in range(2)]
            xh3 = sb3("xh3", [128, E], F32)
            junk3 = sb3("junk3", [128, E], F32)
            st3 = sb3("st3", [128, 16], F32)
            st3qa = sb3("st3qa", [128, 8], F32)
            st3qb = sb3("st3qb", [128, 8], F32)
            pso = [p3.enter_context(nc.psum_tensor(f"pso{i}", [128, 2048], F32))
                   for i in range(2)]

            wait_marks(Y, m2)
            Y(lambda eng, v=cc_heads: eng.wait_ge(sem["sCc"], v))
            Y(lambda eng, v=wout_done: eng.wait_ge(sem["sWo"], v))
            def _mkq(eng):
                q0 = eng.partition_id() % 4
                rt["qidx"] = [eng.compute_val(q0 * 4 + uu) for uu in range(4)]
            Y(_mkq)
            wait_marks(T, m2)
            T(lambda eng, v=wout_done: eng.wait_ge(sem["sWo"], v))
            wait_marks(Sc, m2)
            wait_marks(V, m2)

            ctx_ld = [0, 0]
            ctx16_free = [0, 0]
            x1_free = [0, 0]
            pso_free = [0, 0]
            ao_free = 0
            x1t_sc = [0, 0]
            tile_tov = {}
            for t in range(16):
                u, tb = t // 4, t % 2
                ub = (t // 4) % 2
                if t % 4 == 0:
                    if ctx16_free[ub]:
                        Y(lambda eng, v=ctx16_free[ub]: eng.wait_ge(sem["sT"], v))
                    for i4 in range(4):
                        inc(f"sC16{ub}", 16)
                        Y(lambda eng, i4=i4, u=u, ub=ub, h=sem[f"sC16{ub}"]:
                          eng.dma_start(
                              out=ctx16[ub][:, 4 * i4:4 * (i4 + 1), :],
                              in_=ctxg_d.ap()[:, i4, :, ts(rt["qidx"][u], 512)
                                              ].transpose([1, 0, 2])
                          ).then_inc(h, 16))
                    ctx_ld[ub] = cur(f"sC16{ub}")
                if x1_free[tb]:
                    Y(lambda eng, v=x1_free[tb]: eng.wait_ge(sem["sV"], v))
                inc(f"sXo{tb}", 16)
                Y(lambda eng, t=t, tb=tb, h=sem[f"sXo{tb}"]: eng.dma_start(
                    out=xot[tb][:, :], in_=xo_d.ap()[128 * t:128 * (t + 1), :]
                ).then_inc(h, 16))
                xv = cur(f"sXo{tb}")

                T(lambda eng, v=ctx_ld[ub], h=sem[f"sC16{ub}"]: eng.wait_ge(h, v))
                if pso_free[t % 2]:
                    T(lambda eng, v=pso_free[t % 2]: eng.wait_ge(sem["sV"], v))
                for gH in range(16):
                    for j in range(4):
                        if gH == 15 and j == 3:
                            T(lambda eng, t=t, gH=gH, j=j, ub=ub: eng.matmul(
                                pso[t % 2][:, 512 * j:512 * (j + 1)],
                                ctx16[ub][:, gH, 128 * (t % 4):128 * (t % 4) + 128],
                                Wout[:, gH, 512 * j:512 * (j + 1)],
                                start=False, stop=True).then_inc(sem["sT"], 1))
                        else:
                            T(lambda eng, t=t, gH=gH, j=j, ub=ub: eng.matmul(
                                pso[t % 2][:, 512 * j:512 * (j + 1)],
                                ctx16[ub][:, gH, 128 * (t % 4):128 * (t % 4) + 128],
                                Wout[:, gH, 512 * j:512 * (j + 1)],
                                start=(gH == 0), stop=(gH == 15)))
                tov = inc("sT")
                tile_tov[t] = tov
                if t % 4 == 3:
                    ctx16_free[ub] = tov
                V(lambda eng, v=tov: eng.wait_ge(sem["sT"], v))
                if ao_free:
                    V(lambda eng, v=ao_free: eng.wait_ge(sem["sS"], v))
                vao = inc("sV")
                V(lambda eng, t=t: eng.tensor_add(ao[:, :], pso[t % 2][:, :],
                                                  bout_bc[:, :]).then_inc(sem["sV"], 1))
                pso_free[t % 2] = vao
                Sc(lambda eng, v=vao: eng.wait_ge(sem["sV"], v))
                sq2 = inc("sS")
                Sc(lambda eng: eng.activation(junk3[:, :], ao[:, :], AF.Square,
                                              bias=0.0, scale=1.0,
                                              accum_out=st3qa[:, 0:1]))
                Sc(lambda eng: eng.drain().then_inc(sem["sS"], 1))
                V(lambda eng, v=vao: eng.wait_ge(sem["sV"], v))
                vv2, ac2, bc2 = ln_coeffs(ao[:, :], st3, 0, sq2, st3qa[:, 0:1])
                ao_free = sq2
                V(lambda eng, ac2=ac2, bc2=bc2: eng.tensor_scalar(
                    x2h[:, :], ao[:, :], st3[:, ac2:ac2 + 1],
                    st3[:, bc2:bc2 + 1], ALU.mult, ALU.add))
                V(lambda eng, v=xv, h=sem[f"sXo{tb}"]: eng.wait_ge(h, v))
                if x1t_sc[tb]:
                    V(lambda eng, v=x1t_sc[tb]: eng.wait_ge(sem["sS"], v))
                if t >= 2:
                    V(lambda eng, v=cur(f"sX1{tb}"), h=sem[f"sX1{tb}"]:
                      eng.wait_ge(h, v))
                vx1 = inc("sV")
                V(lambda eng, tb=tb: eng.tensor_add(x1t[tb][:, :], xot[tb][:, :],
                                                    x2h[:, :]).then_inc(sem["sV"], 1))
                Y(lambda eng, v=vx1: eng.wait_ge(sem["sV"], v))
                inc(f"sX1{tb}", 16)
                Y(lambda eng, t=t, tb=tb, h=sem[f"sX1{tb}"]: eng.dma_start(
                    out=x1_d.ap()[128 * t:128 * (t + 1), :], in_=x1t[tb][:, :]
                ).then_inc(h, 16))
                Sc(lambda eng, v=vx1: eng.wait_ge(sem["sV"], v))
                sq3 = inc("sS")
                Sc(lambda eng, tb=tb: eng.activation(junk3[:, :], x1t[tb][:, :],
                                                     AF.Square, bias=0.0, scale=1.0,
                                                     accum_out=st3qb[:, 0:1]))
                Sc(lambda eng: eng.drain().then_inc(sem["sS"], 1))
                vv3, ac3, bc3 = ln_coeffs(x1t[tb][:, :], st3, 8, sq3,
                                          st3qb[:, 0:1])
                x1t_sc[tb] = sq3
                V(lambda eng, t=t: eng.tensor_copy(m3s[:, t:t + 1], st3[:, 8:9]))
                V(lambda eng, t=t, ac3=ac3: eng.tensor_copy(
                    r3s[:, t:t + 1], st3[:, ac3:ac3 + 1]))
                V(lambda eng, tb=tb, ac3=ac3, bc3=bc3: eng.tensor_scalar(
                    xh3[:, :], x1t[tb][:, :], st3[:, ac3:ac3 + 1],
                    st3[:, bc3:bc3 + 1], ALU.mult, ALU.add))
                V(lambda eng: eng.tensor_reduce(st3[:, 14:15], xh3[:, :], AX.X,
                                                ALU.max, apply_absolute_value=True))
                V(Vd)
                x1_free[tb] = inc("sV")
                V(lambda eng: eng.tensor_max(gmax_a[:, :], gmax_a[:, :],
                                             st3[:, 14:15]).then_inc(sem["sV"], 1))
            m3m = barrier()
            flush_block()
        if PHASES < 4:
            wout_scope.close()
            return nc

        with nc.psum_tensor("ptl", [128, 512], F32) as ptl:
            wait_marks(T, m3m)
            ttl = inc("sT")
            T(lambda eng: eng.transpose(ptl[0:1, 0:128], gmax_a[:, :],
                                        ident32[:, :]).then_inc(sem["sT"], 1))
            wait_marks(V, m3m)
            V(lambda eng, v=ttl: eng.wait_ge(sem["sT"], v))
            V(lambda eng: eng.tensor_reduce(gx4[:, 0:1], ptl[0:1, 0:128], AX.X,
                                            ALU.max))
            V(Vd)
            V(lambda eng: eng.tensor_copy(gx4[:, 1:2], gx4[:, 0:1]))
            V(lambda eng: eng.tensor_copy(gx4[:, 2:3], gx4[:, 0:1]))
            vg4 = inc("sV")
            V(lambda eng: eng.tensor_copy(gx4[:, 3:4], gx4[:, 0:1]))
            V(lambda eng: eng.drain().then_inc(sem["sV"], 1))
            wait_marks(G, m3m)
            G(lambda eng, v=vg4: eng.wait_ge(sem["sV"], v))
            vgi = inc("sGm", 16)
            G(lambda eng: eng.dma_start(out=gmi_d.ap(), in_=gx4[:, :]
                                        ).then_inc(sem["sGm"], 16))
            G(lambda eng, v=vgi: eng.wait_ge(sem["sGm"], v))
            vcc = inc("sCg")
            G(lambda eng: eng.collective_compute(
                "AllGather", ALU.bypass,
                replica_groups=[[0, 1, 2, 3, 4, 5, 6, 7]],
                ins=[gmi_d.ap()], outs=[gmo_d.ap()]).then_inc(sem["sCg"], 1))
            wait_marks(Y, m3m)
            Y(lambda eng, v=vcc: eng.wait_ge(sem["sCg"], v))
            vgl = inc("sGx", 16)
            Y(lambda eng: eng.dma_start(
                out=gxall[:, :], in_=gmo_d.ap().rearrange("a b c -> b (a c)")
            ).then_inc(sem["sGx"], 16))
            V(lambda eng, v=vgl: eng.wait_ge(sem["sGx"], v))
            gx_s0 = gxall[:, :].rearrange("a (c s) -> a s c", s=4)[:, 0, :]
            V(lambda eng: eng.tensor_reduce(gx4[:, 0:1], gx_s0, AX.X, ALU.max))
            V(Vd)
            V(lambda eng: eng.tensor_scalar(gx4[:, 0:1], gx4[:, 0:1], 1.0 / 448.0,
                                            1e-7, ALU.mult, ALU.max))
            V(Vd)
            V(lambda eng: eng.reciprocal(gx4[:, 1:2], gx4[:, 0:1]))
            V(Vd)
            V(lambda eng: eng.tensor_tensor(gx4[:, 2:3], gx4[:, 0:1], gx4[:, 1:2],
                                            ALU.mult))
            V(Vd)
            V(lambda eng: eng.tensor_scalar(gx4[:, 2:3], gx4[:, 2:3], -1.0, 2.0,
                                            ALU.mult, ALU.add))
            V(Vd)
            V(lambda eng: eng.tensor_tensor(gx4[:, 1:2], gx4[:, 1:2], gx4[:, 2:3],
                                            ALU.mult))
            V(Vd)
            vsc = inc("sV")
            V(lambda eng: eng.tensor_scalar_mul(gx4[:, 1:2], gx4[:, 1:2], 0.5))
            V(lambda eng: eng.drain().then_inc(sem["sV"], 1))
            T(lambda eng, v=vsc: eng.wait_ge(sem["sV"], v))
            tsc = inc("sT")
            T(lambda eng: eng.matmul(ptl[:, 0:1], ones32[:, :], gx4[0:1, 1:2],
                                     start=True, stop=True).then_inc(sem["sT"], 1))
            V(lambda eng, v=tsc: eng.wait_ge(sem["sT"], v))
            V(lambda eng: eng.tensor_copy(scale_b[:, :], ptl[:, 0:1]))
            V(Vd)
            for t in range(16):
                V(lambda eng, t=t: eng.tensor_tensor(aqs[:, t:t + 1], r3s[:, t:t + 1],
                                                     scale_b[:, :], ALU.mult))
            V(Vd)
            for t in range(16):
                V(lambda eng, t=t: eng.scalar_tensor_tensor(
                    bqs[:, t:t + 1], m3s[:, t:t + 1], -1.0, aqs[:, t:t + 1],
                    ALU.mult, ALU.mult))
            inc("sV")
            V(lambda eng: eng.drain().then_inc(sem["sV"], 1))
            m3t = barrier()
            flush_block()
        wout_scope.close()
        if PHASES < 5:
            return nc

        with ExitStack() as p4:
            def sb4(n, sh, dt):
                return p4.enter_context(nc.sbuf_tensor(n, sh, dt))

            wT = sb4("wT", [128, 16, E], BF16)
            x1i = [sb4(f"x1i{i}", [128, E], F32) for i in range(2)]
            xqt = sb4("xqt", [128, E], F32)
            xq8 = sb4("xq8", [128, E], FP8)
            xqbf = sb4("xqbf", [128, E], BF16)
            xqf = sb4("xqf", [128, 16, 128], BF16)
            ffo = sb4("ffo", [128, E], F32)
            x4h = sb4("x4h", [128, E], F32)
            junk4 = sb4("junk4", [128, E], F32)
            st4 = sb4("st4", [128, 8], F32)
            st4q = sb4("st4q", [128, 8], F32)
            outt = [sb4(f"outt{i}", [128, E], F32) for i in range(2)]
            psf = p4.enter_context(nc.psum_tensor("psf", [128, 2048], F32))
            ptb4 = [p4.enter_context(nc.psum_tensor(f"ptb4{i}", [128, 1024], BF16))
                    for i in range(2)]

            wait_marks(G, m3t)
            for ic in range(16):
                inc("sWf", 16)
                G(lambda eng, ic=ic: eng.dma_start(
                    out=wT[:, ic, :], in_=wt_d.ap()[128 * ic:128 * (ic + 1), :]
                ).then_inc(sem["sWf"], 16))
            wt_done = cur("sWf")

            wait_marks(Y, m3t)
            Y(lambda eng, v=cur("sX10"): eng.wait_ge(sem["sX10"], v))
            Y(lambda eng, v=cur("sX11"): eng.wait_ge(sem["sX11"], v))
            wait_marks(T, m3t)
            wait_marks(Sc, m3t)
            wait_marks(V, m3t)
            T(lambda eng, v=wt_done: eng.wait_ge(sem["sWf"], v))

            x1_ld = [0, 0]
            out_free = [0, 0]
            psf_free = 0
            ptb_free = [0, 0]
            xq_free = 0
            xqt_free = 0
            ffo_free = 0
            for t in range(16):
                tb = t % 2
                if x1_ld[tb]:
                    Y(lambda eng, v=x1_ld[tb]: eng.wait_ge(sem["sV"], v))
                inc(f"sXl{tb}", 16)
                Y(lambda eng, t=t, tb=tb, h=sem[f"sXl{tb}"]: eng.dma_start(
                    out=x1i[tb][:, :], in_=x1_d.ap()[128 * t:128 * (t + 1), :]
                ).then_inc(h, 16))
                xv = cur(f"sXl{tb}")
                V(lambda eng, v=xv, h=sem[f"sXl{tb}"]: eng.wait_ge(h, v))
                if xqt_free:
                    V(lambda eng, v=xqt_free: eng.wait_ge(sem["sS"], v))
                tsv = inc("sV")
                V(lambda eng, t=t, tb=tb: eng.tensor_scalar(
                    xqt[:, :], x1i[tb][:, :], aqs[:, t:t + 1], bqs[:, t:t + 1],
                    ALU.mult, ALU.add).then_inc(sem["sV"], 1))
                Sc(lambda eng, v=tsv: eng.wait_ge(sem["sV"], v))
                xqt_free = inc("sS")
                Sc(lambda eng: eng.copy(xq8[:, :], xqt[:, :]).then_inc(sem["sS"], 1))
                if xq_free:
                    Sc(lambda eng, v=xq_free: eng.wait_ge(sem["sT"], v))
                sqv = inc("sS")
                Sc(lambda eng: eng.copy(xqbf[:, :], xq8[:, :]).then_inc(sem["sS"], 1))
                T(lambda eng, v=sqv: eng.wait_ge(sem["sS"], v))
                for half in range(2):
                    if ptb_free[half]:
                        T(lambda eng, v=ptb_free[half]: eng.wait_ge(sem["sV"], v))
                    tv = inc("sT")
                    for j in range(8):
                        jj = half * 8 + j
                        if j < 7:
                            T(lambda eng, half=half, j=j, jj=jj: eng.transpose(
                                ptb4[half][:, 128 * j:128 * (j + 1)],
                                xqbf[:, 128 * jj:128 * (jj + 1)], ident_bf[:, :]))
                        else:
                            T(lambda eng, half=half, j=j, jj=jj: eng.transpose(
                                ptb4[half][:, 128 * j:128 * (j + 1)],
                                xqbf[:, 128 * jj:128 * (jj + 1)], ident_bf[:, :]
                            ).then_inc(sem["sT"], 1))
                    V(lambda eng, v=tv: eng.wait_ge(sem["sT"], v))
                    pv = inc("sV")
                    V(lambda eng, half=half: eng.tensor_copy(
                        xqf[:, 8 * half:8 * (half + 1), :].rearrange("p a b -> p (a b)"),
                        ptb4[half][:, :]).then_inc(sem["sV"], 1))
                    ptb_free[half] = pv
                xq_free = cur("sT")
                vxq = cur("sV")
                T(lambda eng, v=vxq: eng.wait_ge(sem["sV"], v))
                if psf_free:
                    T(lambda eng, v=psf_free: eng.wait_ge(sem["sV"], v))
                for ic in range(16):
                    for j in range(4):
                        if ic == 15 and j == 3:
                            T(lambda eng, ic=ic, j=j: eng.matmul(
                                psf[:, 512 * j:512 * (j + 1)], xqf[:, ic, :],
                                wT[:, ic, 512 * j:512 * (j + 1)],
                                start=False, stop=True).then_inc(sem["sT"], 1))
                        else:
                            T(lambda eng, ic=ic, j=j: eng.matmul(
                                psf[:, 512 * j:512 * (j + 1)], xqf[:, ic, :],
                                wT[:, ic, 512 * j:512 * (j + 1)],
                                start=(ic == 0), stop=(ic == 15)))
                tf = inc("sT")
                V(lambda eng, v=tf: eng.wait_ge(sem["sT"], v))
                if ffo_free:
                    V(lambda eng, v=ffo_free: eng.wait_ge(sem["sS"], v))
                psf_free = inc("sV")
                V(lambda eng: eng.scalar_tensor_tensor(
                    ffo[:, :], psf[:, :], wsb[:, :], fb_bc[:, :], ALU.mult, ALU.add
                ).then_inc(sem["sV"], 1))
                Sc(lambda eng, v=psf_free: eng.wait_ge(sem["sV"], v))
                sq4 = inc("sS")
                Sc(lambda eng: eng.activation(junk4[:, :], ffo[:, :], AF.Square,
                                              bias=0.0, scale=1.0,
                                              accum_out=st4q[:, 0:1]))
                Sc(lambda eng: eng.drain().then_inc(sem["sS"], 1))
                vv4, ac4, bc4 = ln_coeffs(ffo[:, :], st4, 0, sq4, st4q[:, 0:1])
                ffo_free = sq4
                V(lambda eng, ac4=ac4, bc4=bc4: eng.tensor_scalar(
                    x4h[:, :], ffo[:, :], st4[:, ac4:ac4 + 1],
                    st4[:, bc4:bc4 + 1], ALU.mult, ALU.add))
                if out_free[tb]:
                    V(lambda eng, v=out_free[tb], h=sem[f"sOu{tb}"]:
                      eng.wait_ge(h, v))
                x1_ld[tb] = inc("sV")
                V(lambda eng, tb=tb: eng.tensor_add(outt[tb][:, :], x1i[tb][:, :],
                                                    x4h[:, :]).then_inc(sem["sV"], 1))
                Y(lambda eng, v=x1_ld[tb]: eng.wait_ge(sem["sV"], v))
                ov = inc(f"sOu{tb}", 16)
                Y(lambda eng, t=t, tb=tb, h=sem[f"sOu{tb}"]: eng.dma_start(
                    out=out_d.ap()[128 * t:128 * (t + 1), :], in_=outt[tb][:, :]
                ).then_inc(h, 16))
                out_free[tb] = ov
            Y(lambda eng, v=cur("sOu0"): eng.wait_ge(sem["sOu0"], v))
            Y(lambda eng, v=cur("sOu1"): eng.wait_ge(sem["sOu1"], v))
            flush_block()

    return nc


def kernel(L0_attn_x, L0_attn_W_qkv, L0_attn_b_qkv, L0_attn_W_out, L0_attn_b_out,
           L0_ffn_weight, L0_ffn_weight_scale, L0_ffn_bias):
    x = np.ascontiguousarray(np.asarray(L0_attn_x, dtype=np.float32))
    wq = np.ascontiguousarray(np.asarray(L0_attn_W_qkv, dtype=np.float32))
    bq = np.asarray(L0_attn_b_qkv, dtype=np.float32)
    wo = np.ascontiguousarray(np.asarray(L0_attn_W_out, dtype=np.float32))
    bo = np.ascontiguousarray(np.asarray(L0_attn_b_out, dtype=np.float32)
                              .reshape(1, E))
    wf = np.asarray(L0_ffn_weight, dtype=np.float32)
    ws = np.ascontiguousarray(np.asarray(L0_ffn_weight_scale, dtype=np.float32)
                              .reshape(1, 1))
    fb = np.ascontiguousarray(np.asarray(L0_ffn_bias, dtype=np.float32)
                              .reshape(1, E))
    wt = np.ascontiguousarray(wf.T)

    in_maps = []
    for c in range(8):
        b, g = c // 4, c % 4
        in_maps.append({
            "xb": x[b],
            "xo": np.ascontiguousarray(x[b, TOK * g:TOK * (g + 1)]),
            "wq": np.ascontiguousarray(wq[:, 1536 * g:1536 * (g + 1)]),
            "bq": np.ascontiguousarray(bq[1536 * g:1536 * (g + 1)].reshape(1, 1536)),
            "wo": wo,
            "bo": bo,
            "wt": wt,
            "fb": fb,
            "ws": ws,
        })
    nc = build()
    res = run_bass_kernel_spmd(nc, in_maps, core_ids=list(range(8)))
    out = np.stack([np.asarray(res.results[c]["out"], dtype=np.float32)
                    for c in range(8)])
    return out.reshape(B, S, E)
